# version 07
# baseline (speedup 1.0000x reference)
import numpy as np
import ml_dtypes
import concourse.bass as bass
import concourse.mybir as mybir
from concourse.bass_utils import run_bass_kernel_spmd

F32 = mybir.dt.float32
BF16 = mybir.dt.bfloat16
I8 = mybir.dt.int8
U8 = mybir.dt.uint8
AF = mybir.ActivationFunctionType
OP = mybir.AluOpType
AX = mybir.AxisListType

P = 128
D = 2048
NDC = 16
INTER = 8192
NIC = 64
NH = 8
NB = 4
T = 4096
NG = 8
KVROWS = 4224
LAMBDA_INIT = 0.8
RMS_EPS = 1.1920929e-07
LN_EPS = 1e-5
ISQRT_HD = float(1.0 / np.sqrt(128.0))
EXP_SHIFT = -12.0

ENGINES = ("sync", "gpsimd", "tensor", "scalar", "vector")
LD, ST, CC = "ld", "st", "cc"
PE, SC, VE, GP = "pe", "sc", "ve", "gp"



def own_blocks(role):
    out = []
    for k in range(8):
        out += [4 * k, 4 * k + 3] if role == 0 else [4 * k + 1, 4 * k + 2]
    return out


def owner_and_local(j):
    m = j % 4
    if m in (0, 3):
        return 0, 2 * (j // 4) + (0 if m == 0 else 1)
    return 1, 2 * (j // 4) + (0 if m == 1 else 1)


def build_masks(role):
    blocks = own_blocks(role)
    m = np.zeros((NG, P, 4, 512), dtype=np.float32)
    for g in range(NG):
        qglob = np.empty(256, dtype=np.int64)
        qglob[:128] = blocks[2 * g] * 128 + np.arange(128)
        qglob[128:] = blocks[2 * g + 1] * 128 + np.arange(128)
        for j in range(4):
            key = (4 * g + j) * 128 + np.arange(128)
            keep = (qglob[None, :] >= key[:, None]).astype(np.float32)
            m[g, :, j, :256] = keep
            m[g, :, j, 256:] = keep
    return np.ascontiguousarray(m.astype(ml_dtypes.bfloat16))


def stage_inputs(inputs):
    x = np.asarray(inputs["L0_attn_x"], dtype=np.float32)
    wq = np.asarray(inputs["L0_attn_wq"], dtype=np.float32)
    wk = np.asarray(inputs["L0_attn_wk"], dtype=np.float32)
    wv = np.asarray(inputs["L0_attn_wv"], dtype=np.float32)
    wo = np.asarray(inputs["L0_attn_wo"], dtype=np.float32)
    wg = np.asarray(inputs["L0_ffn_w_gate"])
    wu = np.asarray(inputs["L0_ffn_w_up"])
    wd = np.asarray(inputs["L0_ffn_w_down"])
    sg = np.asarray(inputs["L0_ffn_s_gate"], dtype=np.float32)
    su = np.asarray(inputs["L0_ffn_s_up"], dtype=np.float32)
    sd = np.asarray(inputs["L0_ffn_s_down"], dtype=np.float32)

    def stage_ofmajor(w):
        a = w.reshape(16, 128, 16, 128)
        return np.ascontiguousarray(a.transpose(0, 3, 2, 1).reshape(16, 128, 2048))

    def stage_gu(w):
        a = w.reshape(16, 128, 64, 128)
        return np.ascontiguousarray(a.transpose(2, 1, 0, 3).reshape(64, 128, 2048))

    def stage_down(w):
        a = w.reshape(64, 128, 16, 128)
        return np.ascontiguousarray(a.transpose(2, 1, 0, 3).reshape(16, 128, 8192))

    wq_s = stage_ofmajor(wq)
    wk_s = stage_ofmajor(wk)
    wo_s = stage_ofmajor(wo)
    wv_s = np.ascontiguousarray(wv.T.reshape(16, 128, 2048))
    wg_s = stage_gu(wg)
    wu_s = stage_gu(wu)
    wd_s = stage_down(wd)
    sg_s = np.ascontiguousarray(sg.reshape(64, 128).T)
    su_s = np.ascontiguousarray(su.reshape(64, 128).T)
    sd_s = np.ascontiguousarray(sd.reshape(16, 128).T)
    lam_s = np.ascontiguousarray(np.concatenate([
        np.asarray(inputs["L0_attn_lq1"], dtype=np.float32),
        np.asarray(inputs["L0_attn_lk1"], dtype=np.float32),
        np.asarray(inputs["L0_attn_lq2"], dtype=np.float32),
        np.asarray(inputs["L0_attn_lk2"], dtype=np.float32),
    ]).reshape(1, 1024))
    masks = [build_masks(0), build_masks(1)]

    in_maps = []
    for c in range(8):
        b, r = c // 2, c % 2
        rows = np.concatenate([np.arange(j * 128, (j + 1) * 128) for j in own_blocks(r)])
        x_own = np.ascontiguousarray(x[b][rows].T).reshape(16, 128, 2048)
        in_maps.append({
            "x": x_own,
            "wq": wq_s, "wk": wk_s, "wv": wv_s, "wo": wo_s,
            "wg": wg_s, "wu": wu_s, "wd": wd_s,
            "sg": sg_s, "su": su_s, "sd": sd_s,
            "lam": lam_s, "mask": masks[r],
        })
    return in_maps


def assemble_output(results):
    out = np.empty((NB, T, D), dtype=np.float32)
    for c in range(8):
        b, r = c // 2, c % 2
        rows = np.concatenate([np.arange(j * 128, (j + 1) * 128) for j in own_blocks(r)])
        out[b][rows] = results[c]["out"].reshape(2048, 2048).T
    return out



class Plan:
    def __init__(self, nc, mark_all_pe=False):
        self.mark_all_pe = mark_all_pe
        self.nc = nc
        self.ops = {e: [] for e in ENGINES}
        self.sems = {}
        self.cnt = {}
        self.dma_streams = {}

    def sem(self, name):
        if name not in self.sems:
            self.sems[name] = self.nc.alloc_semaphore(name)
            self.cnt[name] = 0
        return self.sems[name]

    def dma(self, eng, fn, stream):
        self.dma_streams.setdefault(stream, eng)
        return self.run(eng, fn, inc=stream, amt=16, dma=True)

    def wait(self, eng, name, val=None):
        s = self.sem(name)
        v = self.cnt[name] if val is None else val
        if v <= 0:
            return
        self.ops[eng].append(lambda e, s=s, v=v: e.wait_ge(s, v))

    CHAIN = {"vector": VE, "scalar": SC, "gpsimd": GP}

    def run(self, eng, fn, inc=None, amt=1, dma=False):
        if (eng == "tensor" and self.mark_all_pe and not dma and inc is None):
            inc = PE
        ch = None if dma else self.CHAIN.get(eng)
        if ch is not None:
            assert inc is None or inc == ch, (eng, inc)
            s = self.sem(ch)
            prev = self.cnt[ch]
            self.cnt[ch] += 1
            def op(e, fn=fn, s=s, prev=prev):
                if prev > 0:
                    e.wait_ge(s, prev)
                fn(e).then_inc(s, 1)
            self.ops[eng].append(op)
            return self.cnt[ch]
        if inc is not None:
            s = self.sem(inc)
            self.cnt[inc] += amt
            self.ops[eng].append(lambda e, fn=fn, s=s, a=amt: fn(e).then_inc(s, a))
            return self.cnt[inc]
        self.ops[eng].append(fn)
        return None

    def barrier(self, flushers):
        for stream, eng in self.dma_streams.items():
            self.wait(eng, stream)
        if self.cnt.get(CC, 0):
            self.wait("gpsimd", CC)
        s = self.sem("bar")
        self.wait("tensor", PE)
        for eng in ENGINES:
            ch = self.CHAIN.get(eng)
            if ch is not None:
                self.wait(eng, ch)
            self.cnt["bar"] += 1
            f = flushers.get(eng)
            if f is None:
                self.ops[eng].append(lambda e, s=s: e.nop().then_inc(s, 1))
            else:
                self.ops[eng].append(lambda e, s=s, f=f: f(e).then_inc(s, 1))
        tgt = self.cnt["bar"]
        for eng in ENGINES:
            self.ops[eng].append(lambda e, s=s, t=tgt: e.wait_ge(s, t))


class Carver:
    def __init__(self, nc, name, nbytes):
        self.t = nc.alloc_sbuf_tensor(name, [P, nbytes], U8)
        self.nbytes = nbytes

    def view(self, off, free_shape, dtype):
        n = int(np.prod(free_shape)) * mybir.dt.size(dtype)
        assert off + n <= self.nbytes, (off, n, self.nbytes)
        v = self.t[:, off:off + n].bitcast(dtype)
        if len(free_shape) == 1:
            return v
        if len(free_shape) == 2:
            return v.rearrange("p (a b) -> p a b", a=free_shape[0])
        if len(free_shape) == 3:
            return v.rearrange("p (a b c) -> p a b c", a=free_shape[0], b=free_shape[1])
        raise ValueError(free_shape)



FFNA_PREMM = 0
FFNA_MODE = "normal"
FFNB_LIGHT = set()


def build_program(stop_after="full", mark_all_pe=False):
    nc = bass.Bass("TRN2", target_bir_lowering=False, debug=False)

    d_x = nc.dram_tensor("x", [16, 128, 2048], F32, kind="ExternalInput")
    d_wq = nc.dram_tensor("wq", [16, 128, 2048], F32, kind="ExternalInput")
    d_wk = nc.dram_tensor("wk", [16, 128, 2048], F32, kind="ExternalInput")
    d_wv = nc.dram_tensor("wv", [16, 128, 2048], F32, kind="ExternalInput")
    d_wo = nc.dram_tensor("wo", [16, 128, 2048], F32, kind="ExternalInput")
    d_wg = nc.dram_tensor("wg", [64, 128, 2048], I8, kind="ExternalInput")
    d_wu = nc.dram_tensor("wu", [64, 128, 2048], I8, kind="ExternalInput")
    d_wd = nc.dram_tensor("wd", [16, 128, 8192], I8, kind="ExternalInput")
    d_sg = nc.dram_tensor("sg", [128, 64], F32, kind="ExternalInput")
    d_su = nc.dram_tensor("su", [128, 64], F32, kind="ExternalInput")
    d_sd = nc.dram_tensor("sd", [128, 16], F32, kind="ExternalInput")
    d_lam = nc.dram_tensor("lam", [1, 1024], F32, kind="ExternalInput")
    d_mask = nc.dram_tensor("mask", [8, 128, 4, 512], BF16, kind="ExternalInput")
    d_out = nc.dram_tensor("out", [16, 128, 2048], F32, kind="ExternalOutput")

    d_stream = nc.dram_tensor("stream", [16, 128, 2048], F32)
    d_fuse = nc.dram_tensor("fuse", [4, 128, 64, 512], BF16)
    d_kvst = nc.dram_tensor("kvst", [KVROWS, 2048], BF16)
    d_kvg = nc.dram_tensor("kvg", [2 * KVROWS, 2048], BF16)
    d_qst = nc.dram_tensor("qst", [2048, 2048], BF16)
    d_rq = nc.dram_tensor("rq", [16, 2048], BF16)
    d_rows = nc.dram_tensor("rows", [2, 2048], F32)
    d_finrows = nc.dram_tensor("finrows", [2, 512], BF16)

    A = Carver(nc, "arenaA", 65536)
    W = Carver(nc, "arenaW", 81920)
    lnx = A.view(0, [16, 2048], BF16)
    fuse_sl = A.view(0, [64, 512], BF16)
    yf = A.view(0, [16, 2048], BF16)

    ones_bf = nc.alloc_sbuf_tensor("ones_bf", [P, 1], BF16)
    sexp_col = nc.alloc_sbuf_tensor("sexp_col", [P, 1], F32)
    bexp_col = nc.alloc_sbuf_tensor("bexp_col", [P, 1], F32)
    m05 = nc.alloc_sbuf_tensor("m05", [1, 1], F32)
    lamv = nc.alloc_sbuf_tensor("lamv", [1, 1], F32)
    flushv = nc.alloc_sbuf_tensor("flushv", [1, 4], F32)
    sg_sb = nc.alloc_sbuf_tensor("sg_sb", [P, 64], F32)
    th_sb = nc.alloc_sbuf_tensor("th_sb", [P, 64], F32)
    su_sb = nc.alloc_sbuf_tensor("su_sb", [P, 64], F32)
    sd05 = nc.alloc_sbuf_tensor("sd05", [P, 16], F32)

    psum = nc.alloc_psum_tensor("psum", [P, 4096], F32)

    plan = Plan(nc, mark_all_pe=mark_all_pe)
    FLUSH = {
        "vector": lambda e: e.memset(flushv[0:1, 0:1], 0.0),
        "gpsimd": lambda e: e.memset(flushv[0:1, 1:2], 0.0),
        "scalar": lambda e: e.memzero(flushv[0:1, 2:3]),
    }

    def barrier():
        plan.barrier(FLUSH)

    lam_sb = W.view(0, [1024], F32)
    lrow = W.view(4096, [256], F32)
    lrow2 = W.view(5120, [256], F32)
    lr1 = W.view(6144, [1], F32)
    lr2 = W.view(6176, [1], F32)

    plan.dma("sync", lambda e: e.dma_start(out=sg_sb[:, :], in_=d_sg[:, :]), "pld")
    plan.dma("sync", lambda e: e.dma_start(out=su_sb[:, :], in_=d_su[:, :]), "pld")
    plan.dma("sync", lambda e: e.dma_start(out=sd05[:, :], in_=d_sd[:, :]), "pld")
    plan.dma("sync", lambda e: e.dma_start(out=lam_sb[0:1, :], in_=d_lam[:, :]), "pld")
    plan.wait("vector", "pld")
    plan.run("vector", lambda e: e.memset(ones_bf[:, :], 1.0))
    plan.run("vector", lambda e: e.memset(sexp_col[:, :], ISQRT_HD))
    plan.run("vector", lambda e: e.memset(bexp_col[:, :], EXP_SHIFT))
    plan.run("vector", lambda e: e.memset(m05[:, :], -0.5))
    plan.run("vector", lambda e: e.tensor_scalar_mul(sd05[:, :], sd05[:, :], 0.5))
    plan.run("vector", lambda e: e.reciprocal(th_sb[:, :], sg_sb[:, :]))
    plan.run("vector", lambda e: e.tensor_scalar_mul(th_sb[:, :], th_sb[:, :], -45.0))
    plan.run("vector", lambda e: e.tensor_mul(lrow[0:1, :], lam_sb[0:1, 0:256], lam_sb[0:1, 256:512]))
    plan.run("vector", lambda e: e.tensor_mul(lrow2[0:1, :], lam_sb[0:1, 512:768], lam_sb[0:1, 768:1024]))
    plan.run("vector", lambda e: e.reduce_sum(lr1[0:1, :], lrow[0:1, :], axis=AX.X))
    plan.run("vector", lambda e: e.reduce_sum(lr2[0:1, :], lrow2[0:1, :], axis=AX.X), inc=VE)
    plan.wait("scalar", VE)
    plan.run("scalar", lambda e: e.activation(lr1[0:1, :], lr1[0:1, :], AF.Exp))
    plan.run("scalar", lambda e: e.activation(lr2[0:1, :], lr2[0:1, :], AF.Exp), inc=SC)
    plan.wait("vector", SC)
    plan.run("vector", lambda e: e.tensor_sub(lamv[0:1, :], lr1[0:1, :], lr2[0:1, :]))
    plan.run("vector", lambda e: e.tensor_scalar_add(lamv[0:1, :], lamv[0:1, :], LAMBDA_INIT))
    zpad = W.view(8192, [2048], BF16)
    plan.run("vector", lambda e: e.memset(zpad, 0.0), inc=VE)
    plan.wait("gpsimd", VE)
    plan.dma("gpsimd", lambda e: e.dma_start(out=d_kvst.ap()[4112:4224, :],
                                             in_=zpad[0:112, :]), "pld2")
    barrier()

    def emit_ln(src):
        t_x = [W.view(0, [2048], F32), W.view(8192, [2048], F32)]
        t_xb = [W.view(16384, [2048], BF16), W.view(20480, [2048], BF16)]
        t_sq = [W.view(24576, [2048], BF16), W.view(28672, [2048], BF16)]
        r_a = W.view(32768, [2048], F32)
        r_b = W.view(40960, [2048], F32)
        r_c = W.view(49152, [2048], F32)
        bc_mean = W.view(57344, [2048], F32)
        bc_rstd = W.view(65536, [2048], F32)
        ps_sum = psum[0:1, 0:2048]
        ps_sq = psum[0:1, 2048:4096]

        for dc in range(NDC):
            par = dc % 2
            if dc >= 2:
                plan.wait("sync", VE)
                plan.wait("sync", SC)
            plan.dma("sync", lambda e, dc=dc, par=par:
                     e.dma_start(out=t_x[par], in_=src[dc]), "tx%d" % par)
            plan.wait("vector", "tx%d" % par)
            if dc >= 2:
                plan.wait("vector", PE)
            plan.run("vector", lambda e, par=par: e.tensor_copy(t_xb[par], t_x[par]), inc=VE)
            plan.wait("scalar", "tx%d" % par)
            if dc >= 2:
                plan.wait("scalar", PE)
            plan.run("scalar", lambda e, par=par:
                     e.activation(t_sq[par], t_x[par], AF.Square), inc=SC)
            plan.wait("tensor", VE)
            plan.wait("tensor", SC)
            first, last = dc == 0, dc == NDC - 1
            for half in range(4):
                sl = slice(half * 512, (half + 1) * 512)
                plan.run("tensor", lambda e, par=par, sl=sl, first=first, last=last:
                         e.matmul(ps_sum[:, sl], ones_bf[:, :], t_xb[par][:, sl],
                                  start=first, stop=last))
            for half in range(4):
                sl = slice(half * 512, (half + 1) * 512)
                plan.run("tensor", lambda e, par=par, sl=sl, first=first, last=last:
                         e.matmul(ps_sq[:, sl], ones_bf[:, :], t_sq[par][:, sl],
                                  start=first, stop=last),
                         inc=(PE if (last and half == 3) else None))
        plan.wait("vector", PE)
        plan.run("vector", lambda e: e.tensor_scalar_mul(r_a[0:1, :], ps_sum, 1.0 / D))
        plan.run("vector", lambda e: e.tensor_scalar_mul(r_b[0:1, :], ps_sq, 1.0 / D))
        plan.run("vector", lambda e: e.tensor_mul(r_c[0:1, :], r_a[0:1, :], r_a[0:1, :]))
        plan.run("vector", lambda e: e.tensor_sub(r_b[0:1, :], r_b[0:1, :], r_c[0:1, :]))
        plan.run("vector", lambda e: e.tensor_scalar_add(r_b[0:1, :], r_b[0:1, :], LN_EPS),
                 inc=VE)
        plan.wait("scalar", VE)
        plan.run("scalar", lambda e: e.activation(r_c[0:1, :], r_b[0:1, :], AF.Ln))
        plan.run("scalar", lambda e: e.activation(r_b[0:1, :], r_c[0:1, :], AF.Exp,
                                                  scale=m05[0:1, :]), inc=SC)
        plan.wait("gpsimd", SC)
        plan.dma("gpsimd", lambda e: e.dma_start(out=d_rows[0:1, :], in_=r_a[0:1, :]),
                 "lnrow")
        plan.dma("gpsimd", lambda e: e.dma_start(out=d_rows[1:2, :], in_=r_b[0:1, :]),
                 "lnrow")
        plan.wait("gpsimd", "lnrow")
        plan.dma("gpsimd", lambda e: e.dma_start(
            out=bc_mean, in_=d_rows[0:1, :].broadcast_to([P, 2048])), "lnbc")
        plan.dma("gpsimd", lambda e: e.dma_start(
            out=bc_rstd, in_=d_rows[1:2, :].broadcast_to([P, 2048])), "lnbc")
        for dc in range(NDC):
            par = dc % 2
            plan.wait("sync", VE)
            if dc < 2:
                plan.wait("sync", SC)
            plan.dma("sync", lambda e, dc=dc, par=par:
                     e.dma_start(out=t_x[par], in_=src[dc]), "tx%d" % par)
            plan.wait("vector", "tx%d" % par)
            plan.wait("vector", "lnbc")
            plan.run("vector", lambda e, par=par: e.tensor_sub(t_x[par], t_x[par], bc_mean))
            plan.run("vector", lambda e, dc=dc, par=par:
                     e.tensor_mul(lnx[:, dc, :], t_x[par], bc_rstd), inc=VE)
        barrier()

    def emit_ffn_a():
        w8_g = [W.view(0, [2048], I8), W.view(2048, [2048], I8)]
        w8_u = [W.view(4096, [2048], I8), W.view(6144, [2048], I8)]
        wbf_g = [W.view(57344, [2048], BF16), W.view(61440, [2048], BF16),
                 W.view(65536, [2048], BF16), W.view(69632, [2048], BF16)]
        wbf_u = [W.view(16384, [2048], BF16), W.view(20480, [2048], BF16),
                 W.view(73728, [2048], BF16), W.view(77824, [2048], BF16)]
        g_act = [W.view(24576, [2048], BF16), W.view(28672, [2048], BF16)]
        t_gate = [W.view(40960, [2048], F32), W.view(49152, [2048], F32)]
        fuse_bf = [W.view(32768, [2048], BF16), W.view(36864, [2048], BF16)]
        ps_g = psum[:, 0:2048]
        ps_u = psum[:, 2048:4096]

        for i in range(FFNA_PREMM):
            plan.run("tensor", lambda e: e.matmul(
                psum[:, 0:512], lnx[:, 0, 0:128],
                lnx[:, 1, 0:512], start=True, stop=True))
        for ic in range(NIC):
            par = ic % 2
            par4 = ic % 4
            if ic >= 2:
                plan.wait("sync", GP)
            plan.dma("sync", lambda e, ic=ic, par=par:
                     e.dma_start(out=w8_g[par], in_=d_wg[ic]), "w8%d" % par)
            plan.dma("sync", lambda e, ic=ic, par=par:
                     e.dma_start(out=w8_u[par], in_=d_wu[ic]), "w8%d" % par)
            plan.wait("gpsimd", "w8%d" % par)
            if ic >= 2:
                plan.wait("gpsimd", PE)
            plan.run("gpsimd", lambda e, par=par, par4=par4:
                     e.tensor_copy(wbf_g[par4], w8_g[par]), inc=GP)
            plan.run("gpsimd", lambda e, par=par, par4=par4:
                     e.tensor_copy(wbf_u[par4], w8_u[par]), inc=GP)
            plan.wait("tensor", GP)
            if ic >= 1:
                plan.wait("tensor", VE)
            for dc in range(NDC):
                for ts in range(4):
                    sl = slice(ts * 512, (ts + 1) * 512)
                    plan.run("tensor", lambda e, par4=par4, dc=dc, sl=sl:
                             e.matmul(ps_g[:, sl], wbf_g[par4][:, dc * 128:(dc + 1) * 128],
                                      lnx[:, dc, sl], start=dc == 0, stop=dc == NDC - 1),
                             inc=(PE if (dc == NDC - 1 and ts == 3) else None))
            gate_mark = plan.cnt[PE]
            if ic >= 1:
                plan.wait("tensor", VE)
            for dc in range(NDC):
                for ts in range(4):
                    sl = slice(ts * 512, (ts + 1) * 512)
                    plan.run("tensor", lambda e, par4=par4, dc=dc, sl=sl:
                             e.matmul(ps_u[:, sl], wbf_u[par4][:, dc * 128:(dc + 1) * 128],
                                      lnx[:, dc, sl], start=dc == 0, stop=dc == NDC - 1),
                             inc=(PE if (dc == NDC - 1 and ts == 3) else None))
            plan.wait("vector", PE, gate_mark)
            if ic >= 2:
                plan.wait("vector", SC)
            plan.run("vector", lambda e, ic=ic, par=par:
                     e.tensor_scalar(t_gate[par], ps_g, th_sb[:, ic:ic + 1],
                                     sg_sb[:, ic:ic + 1], OP.max, OP.mult), inc=VE)
            clamp_mark = plan.cnt[VE]
            plan.wait("scalar", VE, clamp_mark)
            if ic >= 2:
                plan.wait("scalar", VE)
            plan.run("scalar", lambda e, par=par:
                     e.activation(g_act[par], t_gate[par], AF.Silu), inc=SC)
            plan.wait("vector", PE)
            plan.wait("vector", SC)
            if ic >= 2:
                plan.wait("vector", "fb%d" % par)
            if FFNA_MODE == "graw":
                plan.run("vector", lambda e, ic=ic, par=par:
                         e.tensor_copy(fuse_bf[par], t_gate[par]), inc=VE)
            elif FFNA_MODE == "gact":
                plan.run("vector", lambda e, ic=ic, par=par:
                         e.tensor_copy(fuse_bf[par], g_act[par]), inc=VE)
            elif FFNA_MODE == "up":
                plan.run("vector", lambda e, ic=ic, par=par:
                         e.tensor_scalar(fuse_bf[par], ps_u, su_sb[:, ic:ic + 1], None,
                                         OP.mult), inc=VE)
            else:
                plan.run("vector", lambda e, ic=ic, par=par:
                         e.scalar_tensor_tensor(fuse_bf[par], ps_u, su_sb[:, ic:ic + 1],
                                                g_act[par], OP.mult, OP.mult), inc=VE)
            plan.wait("sync", VE)
            for tb in range(4):
                plan.dma("sync", lambda e, ic=ic, par=par, tb=tb:
                         e.dma_start(out=d_fuse[tb][:, ic, :],
                                     in_=fuse_bf[par][:, tb * 512:(tb + 1) * 512]),
                         "fb%d" % par)
        barrier()

    def emit_ffn_b(res_src, dst, tb_lim=4, do_lim=16):
        wd8 = [W.view(0, [8192], I8), W.view(8192, [8192], I8)]
        wdbf = [W.view(16384, [8192], BF16), W.view(32768, [8192], BF16)]
        t_res = [W.view(49152, [512], F32), W.view(51200, [512], F32)]
        t_out = [W.view(53248, [512], F32), W.view(55296, [512], F32)]
        tb_evac_mark = 0
        for tb in range(tb_lim):
            if tb >= 1:
                plan.wait("sync", VE, tb_evac_mark)
            for q in range(4):
                if "fsl" in FFNB_LIGHT:
                    plan.dma("sync", lambda e, tb=tb, q=q: e.dma_start(
                        out=fuse_sl[:, q * 16:(q + 1) * 16, 0:16],
                        in_=d_fuse[tb][:, q * 16:(q + 1) * 16, 0:16]), "fsl")
                else:
                    plan.dma("sync", lambda e, tb=tb, q=q: e.dma_start(
                        out=fuse_sl[:, q * 16:(q + 1) * 16, :],
                        in_=d_fuse[tb][:, q * 16:(q + 1) * 16, :]), "fsl")
            for do in range(do_lim):
                par = do % 2
                idx = tb * NDC + do
                if idx >= 2:
                    plan.wait("sync", GP)
                    plan.wait("sync", VE)
                if "wd" in FFNB_LIGHT:
                    plan.dma("sync", lambda e, do=do, par=par:
                             e.dma_start(out=wd8[par][:, 0:128], in_=d_wd[do][:, 0:128]),
                             "wd8%d" % par)
                else:
                    plan.dma("sync", lambda e, do=do, par=par:
                             e.dma_start(out=wd8[par], in_=d_wd[do]), "wd8%d" % par)
                if "res" in FFNB_LIGHT:
                    plan.dma("sync", lambda e, do=do, tb=tb, par=par:
                             e.dma_start(out=t_res[par][:, 0:16],
                                         in_=res_src[do][:, 0:16]), "tr%d" % par)
                else:
                    plan.dma("sync", lambda e, do=do, tb=tb, par=par:
                             e.dma_start(out=t_res[par],
                                         in_=res_src[do][:, tb * 512:(tb + 1) * 512]),
                             "tr%d" % par)
                plan.wait("gpsimd", "wd8%d" % par)
                if idx >= 2:
                    plan.wait("gpsimd", PE)
                if "conv" in FFNB_LIGHT:
                    plan.run("gpsimd", lambda e, par=par:
                             e.tensor_copy(wdbf[par][:, 0:16], wd8[par][:, 0:16]), inc=GP)
                else:
                    plan.run("gpsimd", lambda e, par=par:
                             e.tensor_copy(wdbf[par][:, 0:4096], wd8[par][:, 0:4096]),
                             inc=GP)
                plan.wait("vector", "wd8%d" % par)
                if idx >= 2:
                    plan.wait("vector", PE)
                if "conv" in FFNB_LIGHT:
                    plan.run("vector", lambda e, par=par:
                             e.tensor_copy(wdbf[par][:, 4096:4112], wd8[par][:, 4096:4112]),
                             inc=VE)
                else:
                    plan.run("vector", lambda e, par=par:
                             e.tensor_copy(wdbf[par][:, 4096:8192], wd8[par][:, 4096:8192]),
                             inc=VE)
                conv_ve_mark = plan.cnt[VE]
                ps_d = psum[:, par * 512:(par + 1) * 512]
                plan.wait("tensor", GP)
                plan.wait("tensor", VE, conv_ve_mark)
                plan.wait("tensor", "fsl")
                for ich in range(NIC):
                    if "mm" in FFNB_LIGHT:
                        plan.run("tensor", lambda e, par=par, ich=ich, ps_d=ps_d:
                                 e.matmul(ps_d[:, 0:16],
                                          wdbf[par][:, ich * 128:(ich + 1) * 128],
                                          fuse_sl[:, ich, 0:16], start=ich == 0,
                                          stop=ich == NIC - 1),
                                 inc=(PE if ich == NIC - 1 else None))
                    else:
                        plan.run("tensor", lambda e, par=par, ich=ich, ps_d=ps_d:
                                 e.matmul(ps_d, wdbf[par][:, ich * 128:(ich + 1) * 128],
                                          fuse_sl[:, ich, :], start=ich == 0,
                                          stop=ich == NIC - 1),
                                 inc=(PE if ich == NIC - 1 else None))
                plan.wait("vector", PE)
                plan.wait("vector", "tr%d" % par)
                if idx >= 2:
                    plan.wait("vector", "to%d" % par)
                if "evac" in FFNB_LIGHT:
                    plan.run("vector", lambda e, do=do, par=par, ps_d=ps_d:
                             e.scalar_tensor_tensor(t_out[par][:, 0:16], ps_d[:, 0:16],
                                                    sd05[:, do:do + 1],
                                                    t_res[par][:, 0:16],
                                                    OP.mult, OP.add), inc=VE)
                else:
                    plan.run("vector", lambda e, do=do, par=par, ps_d=ps_d:
                             e.scalar_tensor_tensor(t_out[par], ps_d, sd05[:, do:do + 1],
                                                    t_res[par], OP.mult, OP.add), inc=VE)
                plan.wait("sync", VE)
                plan.dma("sync", lambda e, do=do, tb=tb, par=par:
                         e.dma_start(out=dst[do][:, tb * 512:(tb + 1) * 512],
                                     in_=t_out[par]), "to%d" % par)
                if do == do_lim - 1:
                    tb_evac_mark = plan.cnt[VE]
        barrier()

    def emit_proj_qk(w_dram, stash_dst_fn, rrow_dst_fn):
        wsl = [W.view(0, [2048], F32), W.view(8192, [2048], F32)]
        wslb = [W.view(16384, [2048], BF16), W.view(20480, [2048], BF16)]
        sqb = [W.view(24576, [2048], BF16), W.view(28672, [2048], BF16)]
        outb = [W.view(32768, [2048], BF16), W.view(36864, [2048], BF16)]
        r_a = W.view(40960, [2048], F32)
        r_c = W.view(49152, [2048], F32)
        rowbf = W.view(57344, [2048], BF16)
        ps_q = psum[:, 0:2048]
        ps_rr = psum[0:1, 2048:4096]

        for of in range(NDC):
            par = of % 2
            plan.wait("sync", VE)
            plan.dma("sync", lambda e, of=of, par=par:
                     e.dma_start(out=wsl[par], in_=w_dram[of]), "wsl%d" % par)
            plan.wait("vector", "wsl%d" % par)
            if of >= 2:
                plan.wait("vector", PE)
            plan.run("vector", lambda e, par=par: e.tensor_copy(wslb[par], wsl[par]),
                     inc=VE)
            plan.wait("tensor", VE)
            plan.wait("tensor", SC)
            for dc in range(NDC):
                for ts in range(4):
                    sl = slice(ts * 512, (ts + 1) * 512)
                    plan.run("tensor", lambda e, par=par, dc=dc, sl=sl:
                             e.matmul(ps_q[:, sl], wslb[par][:, dc * 128:(dc + 1) * 128],
                                      lnx[:, dc, sl], start=dc == 0, stop=dc == NDC - 1),
                             inc=(PE if (dc == NDC - 1 and ts == 3) else None))
            plan.wait("scalar", PE)
            plan.wait("scalar", "ob%d" % par)
            plan.run("scalar", lambda e, par=par:
                     e.activation(sqb[par], ps_q, AF.Square))
            plan.run("scalar", lambda e, par=par:
                     e.activation(outb[par], ps_q, AF.Copy), inc=SC)
            plan.wait("tensor", SC)
            plan.wait("tensor", VE)
            for half in range(4):
                sl = slice(half * 512, (half + 1) * 512)
                plan.run("tensor", lambda e, par=par, sl=sl:
                         e.matmul(ps_rr[:, sl], ones_bf[:, :], sqb[par][:, sl],
                                  start=True, stop=True),
                         inc=(PE if half == 3 else None))
            plan.wait("vector", PE)
            plan.run("vector", lambda e: e.tensor_scalar_mul(r_a[0:1, :], ps_rr, 1.0 / 128.0))
            plan.run("vector", lambda e: e.tensor_scalar_add(r_a[0:1, :], r_a[0:1, :],
                                                             RMS_EPS), inc=VE)
            plan.wait("scalar", VE)
            plan.run("scalar", lambda e: e.activation(r_c[0:1, :], r_a[0:1, :], AF.Ln))
            plan.run("scalar", lambda e: e.activation(r_a[0:1, :], r_c[0:1, :], AF.Exp,
                                                      scale=m05[0:1, :]), inc=SC)
            plan.wait("vector", SC)
            plan.wait("vector", "rrow")
            plan.run("vector", lambda e: e.tensor_copy(rowbf[0:1, :], r_a[0:1, :]), inc=VE)
            plan.wait("gpsimd", VE)
            plan.wait("gpsimd", SC)
            plan.dma("gpsimd", lambda e, of=of:
                     e.dma_start(out=rrow_dst_fn(of), in_=rowbf[0:1, :]), "rrow")
            plan.dma("gpsimd", lambda e, of=of, par=par:
                     e.dma_start(out=stash_dst_fn(of), in_=outb[par]), "ob%d" % par)

    def emit_vproj():
        wsl = [W.view(0, [2048], F32), W.view(8192, [2048], F32)]
        wv_bf = W.view(61440, [16, 512], BF16)
        vout = [W.view(77824, [512], BF16), W.view(78848, [512], BF16)]
        kv = d_kvst.ap()
        plan.wait("tensor", SC)
        for vq in range(4):
            for dc in range(NDC):
                par = dc % 2
                plan.wait("sync", VE)
                plan.dma("sync", lambda e, dc=dc, vq=vq, par=par:
                         e.dma_start(out=wsl[par][:, 0:512],
                                     in_=d_wv[dc][:, vq * 512:(vq + 1) * 512]),
                         "wsl%d" % par)
                plan.wait("vector", "wsl%d" % par)
                if vq > 0 and dc == 0:
                    plan.wait("vector", PE)
                plan.run("vector", lambda e, dc=dc, par=par:
                         e.tensor_copy(wv_bf[:, dc, :], wsl[par][:, 0:512]), inc=VE)
            plan.wait("tensor", VE)
            for tc in range(NDC):
                par = tc % 2
                ps_v = psum[:, par * 512:(par + 1) * 512]
                if vq * NDC + tc >= 2:
                    plan.wait("tensor", SC)
                for dc in range(NDC):
                    plan.run("tensor", lambda e, tc=tc, dc=dc, ps_v=ps_v:
                             e.matmul(ps_v, lnx[:, dc, tc * 128:(tc + 1) * 128],
                                      wv_bf[:, dc, :], start=dc == 0, stop=dc == NDC - 1),
                             inc=(PE if dc == NDC - 1 else None))
                plan.wait("scalar", PE)
                plan.wait("scalar", "vo%d" % par)
                plan.run("scalar", lambda e, par=par, ps_v=ps_v:
                         e.activation(vout[par], ps_v, AF.Copy), inc=SC)
                plan.wait("gpsimd", SC)
                kvv = kv[2048:4096, :].rearrange("(h p two) f -> h p two f",
                                                 h=8, p=P)
                for sub in range(2):
                    plan.dma("gpsimd", lambda e, tc=tc, vq=vq, par=par, sub=sub, kvv=kvv:
                             e.dma_start(
                                 out=kvv[2 * vq + sub][:, tc // 8,
                                                       (tc % 8) * 256:(tc % 8 + 1) * 256],
                                 in_=vout[par][:, sub * 256:(sub + 1) * 256]),
                             "vo%d" % par)

    def emit_qkv():
        kv = d_kvst.ap()
        emit_proj_qk(d_wk,
                     lambda of: kv[of * 128:(of + 1) * 128, :],
                     lambda of: kv[4096 + of:4097 + of, :])
        emit_vproj()
        for s in ("rrow", "ob0", "ob1", "vo0", "vo1"):
            plan.wait("gpsimd", s)
        plan.run("gpsimd", lambda e: e.collective_compute(
            "AllGather", OP.bypass,
            replica_groups=[[0, 1], [2, 3], [4, 5], [6, 7]],
            ins=[d_kvst.ap().opt()], outs=[d_kvg.ap().opt()]), inc=CC, dma=True)
        q = d_qst.ap()
        emit_proj_qk(d_wq,
                     lambda of: q[of * 128:(of + 1) * 128, :],
                     lambda of: d_rq[of:of + 1, :])
        barrier()

    def emit_attention():
        q1h = W.view(0, [2048], BF16)
        q2h = W.view(4096, [2048], BF16)
        k1h = W.view(8192, [2, 2048], BF16)
        k2h = W.view(16384, [2, 2048], BF16)
        vh = W.view(24576, [2, 16, 256], BF16)
        bc_q = W.view(40960, [2, 2048], BF16)
        bc_k1 = W.view(49152, [2, 2048], BF16)
        bc_k2 = W.view(57344, [2, 2048], BF16)
        maskslot = [W.view(65536, [4, 512], BF16), W.view(69632, [4, 512], BF16)]
        Et = [W.view(73728, [512], BF16), W.view(74752, [512], BF16)]
        bc_fin = W.view(75776, [512], BF16)
        rowb = W.view(76800, [512], BF16)
        rrow = W.view(77824, [512], F32)
        t_a = W.view(79872, [256], F32)
        t_b = W.view(80896, [256], F32)
        kvg = d_kvg.ap()
        kvg_r = kvg.rearrange("(r a) f -> r a f", r=2)
        qst = d_qst.ap()

        exp_marks = []
        pvm_marks = []
        cons_marks = []
        fin_marks = []
        step_idx = 0

        for h in range(NH):
            if h > 0:
                plan.wait("sync", VE, fin_marks[-1])
            plan.dma("sync", lambda e, h=h: e.dma_start(
                out=q1h, in_=qst[h * 256:h * 256 + 128, :]), "attld")
            plan.dma("sync", lambda e, h=h: e.dma_start(
                out=q2h, in_=qst[h * 256 + 128:h * 256 + 256, :]), "attld")
            for r in range(2):
                base = KVROWS * r
                plan.dma("sync", lambda e, h=h, base=base: e.dma_start(
                    out=k1h[:, (base // KVROWS), :],
                    in_=kvg[base + h * 256:base + h * 256 + 128, :]), "attld")
                plan.dma("sync", lambda e, h=h, base=base: e.dma_start(
                    out=k2h[:, (base // KVROWS), :],
                    in_=kvg[base + h * 256 + 128:base + h * 256 + 256, :]), "attld")
                plan.dma("sync", lambda e, h=h, r=r, base=base: e.dma_start(
                    out=vh[:, r, :, :].rearrange("p a f -> p (a f)"),
                    in_=kvg[base + 2048:base + 4096, :]
                        .rearrange("(hh p two) f -> hh p (two f)",
                                   hh=8, p=P)[h]), "attld")
            plan.wait("gpsimd", VE)
            plan.dma("gpsimd", lambda e, h=h: e.dma_start(
                out=bc_q, in_=d_rq[2 * h:2 * h + 2, :].unsqueeze(0)
                .broadcast_to([P, 2, 2048])), "bc")
            plan.dma("gpsimd", lambda e, h=h: e.dma_start(
                out=bc_k1, in_=kvg_r[:, 4096 + 2 * h, :].unsqueeze(0)
                .broadcast_to([P, 2, 2048])), "bc")
            plan.dma("gpsimd", lambda e, h=h: e.dma_start(
                out=bc_k2, in_=kvg_r[:, 4096 + 2 * h + 1, :].unsqueeze(0)
                .broadcast_to([P, 2, 2048])), "bc")
            plan.wait("vector", "attld")
            plan.wait("vector", "bc")
            if h > 0:
                plan.wait("vector", PE)
            plan.run("vector", lambda e: e.tensor_mul(q1h, q1h, bc_q[:, 0, :]))
            plan.run("vector", lambda e: e.tensor_mul(q2h, q2h, bc_q[:, 1, :]))
            plan.run("vector", lambda e: e.tensor_mul(
                k1h.rearrange("p a b -> p (a b)"), k1h.rearrange("p a b -> p (a b)"),
                bc_k1.rearrange("p a b -> p (a b)")))
            plan.run("vector", lambda e: e.tensor_mul(
                k2h.rearrange("p a b -> p (a b)"), k2h.rearrange("p a b -> p (a b)"),
                bc_k2.rearrange("p a b -> p (a b)")), inc=VE)
            norm_mark = plan.cnt[VE]

            for g in range(NG):
                gpar = g % 2
                gi = h * NG + g
                ps_r12 = psum[0:1, 1024 + gpar * 512:1024 + (gpar + 1) * 512]
                ps_y = psum[:, 2048 + gpar * 1024:2048 + (gpar + 1) * 1024]
                if gi >= 2:
                    plan.wait("sync", VE, fin_marks[gi - 2])
                plan.dma("sync", lambda e, g=g, gpar=gpar: e.dma_start(
                    out=maskslot[gpar], in_=d_mask[g]), "mk%d" % gpar)
                mask_ld_mark = plan.cnt["mk%d" % gpar]
                nsteps = 4 * g + 4
                qsl = slice(g * 256, (g + 1) * 256)
                first_group_wait = fin_marks[gi - 2] if gi >= 2 else 0
                for c in range(nsteps):
                    spar = step_idx % 2
                    ps_s = psum[:, spar * 512:(spar + 1) * 512]
                    rr, ll = owner_and_local(c)
                    if step_idx >= 2:
                        plan.wait("tensor", SC, exp_marks[step_idx - 2])
                    if c == 0:
                        plan.wait("tensor", VE, norm_mark)
                        if first_group_wait:
                            plan.wait("tensor", VE, first_group_wait)
                    plan.run("tensor", lambda e, ps_s=ps_s, rr=rr, ll=ll, qsl=qsl:
                             e.matmul(ps_s[:, 0:256], k1h[:, rr, ll * 128:(ll + 1) * 128],
                                      q1h[:, qsl], start=True, stop=True))
                    plan.run("tensor", lambda e, ps_s=ps_s, rr=rr, ll=ll, qsl=qsl:
                             e.matmul(ps_s[:, 256:512], k2h[:, rr, ll * 128:(ll + 1) * 128],
                                      q2h[:, qsl], start=True, stop=True), inc=PE)
                    s_mark = plan.cnt[PE]
                    plan.wait("scalar", PE, s_mark)
                    if step_idx >= 2:
                        plan.wait("scalar", PE, pvm_marks[step_idx - 2])
                        cm = cons_marks[step_idx - 2]
                        if cm[0] == VE:
                            pass
                    plan.run("scalar", lambda e, spar=spar, ps_s=ps_s:
                             e.activation(Et[spar], ps_s, AF.Exp,
                                          scale=sexp_col[:, :], bias=bexp_col[:, :]),
                             inc=SC)
                    exp_marks.append(plan.cnt[SC])
                    if c >= 4 * g:
                        plan.wait("vector", SC, exp_marks[-1])
                        plan.wait("vector", "mk%d" % gpar, mask_ld_mark)
                        plan.run("vector", lambda e, spar=spar, gpar=gpar, j=c - 4 * g:
                                 e.tensor_mul(Et[spar], Et[spar], maskslot[gpar][:, j, :]),
                                 inc=VE)
                        cons_marks.append((VE, plan.cnt[VE]))
                        plan.wait("tensor", VE, plan.cnt[VE])
                    else:
                        cons_marks.append((SC, exp_marks[-1]))
                        plan.wait("tensor", SC, exp_marks[-1])
                    first, last = c == 0, c == nsteps - 1
                    plan.run("tensor", lambda e, spar=spar, ps_r12=ps_r12, first=first,
                             last=last:
                             e.matmul(ps_r12, ones_bf[:, :], Et[spar],
                                      start=first, stop=last))
                    for half in range(2):
                        vsl = slice(half * 128, (half + 1) * 128)
                        plan.run("tensor", lambda e, rr=rr, ll=ll, vsl=vsl, half=half,
                                 spar=spar, ps_y=ps_y, first=first, last=last:
                                 e.matmul(ps_y[:, half * 512:(half + 1) * 512],
                                          vh[:, rr, ll, vsl], Et[spar][:, :],
                                          start=first, stop=last),
                                 inc=(PE if half == 1 else None))
                    pvm_marks.append(plan.cnt[PE])
                    step_idx += 1
                plan.wait("vector", PE, pvm_marks[-1])
                plan.run("vector", lambda e, ps_r12=ps_r12:
                         e.reciprocal(rrow[0:1, 0:256], ps_r12[:, 0:256]))
                plan.run("vector", lambda e, ps_r12=ps_r12:
                         e.reciprocal(rrow[0:1, 256:512], ps_r12[:, 256:512]))
                plan.run("vector", lambda e: e.tensor_scalar(
                    rrow[0:1, 256:512], rrow[0:1, 256:512], lamv[0:1, :], None, OP.mult))
                plan.run("vector", lambda e: e.tensor_copy(rowb[0:1, :], rrow[0:1, :]),
                         inc=VE)
                plan.wait("gpsimd", VE)
                fpar = gi % 2
                plan.dma("gpsimd", lambda e, fpar=fpar: e.dma_start(
                    out=d_finrows[fpar:fpar + 1, :], in_=rowb[0:1, :]), "fr")
                plan.wait("gpsimd", "fr")
                plan.dma("gpsimd", lambda e, fpar=fpar: e.dma_start(
                    out=bc_fin, in_=d_finrows[fpar:fpar + 1, :].broadcast_to([P, 512])),
                    "fr")
                plan.wait("vector", "fr")
                for half in range(2):
                    y1 = 2 * half
                    y2 = 2 * half + 1
                    fc = 2 * h + half
                    plan.run("vector", lambda e, y1=y1, ps_y=ps_y:
                             e.tensor_mul(t_a, ps_y[:, y1 * 256:(y1 + 1) * 256],
                                          bc_fin[:, 0:256]))
                    plan.run("vector", lambda e, y2=y2, ps_y=ps_y:
                             e.tensor_mul(t_b, ps_y[:, y2 * 256:(y2 + 1) * 256],
                                          bc_fin[:, 256:512]))
                    plan.run("vector", lambda e, fc=fc, qsl=qsl:
                             e.tensor_sub(yf[:, fc, qsl], t_a, t_b),
                             inc=(VE if half == 1 else None))
                fin_marks.append(plan.cnt[VE])
        barrier()

    def emit_oproj():
        wsl = [W.view(0, [2048], F32), W.view(8192, [2048], F32)]
        wslb = [W.view(16384, [2048], BF16), W.view(20480, [2048], BF16)]
        st_res = [W.view(24576, [2048], F32), W.view(32768, [2048], F32)]
        st_out = [W.view(40960, [2048], F32), W.view(49152, [2048], F32)]
        evac_marks = []
        for do in range(NDC):
            par = do % 2
            if do >= 2:
                plan.wait("sync", VE)
            plan.dma("sync", lambda e, do=do, par=par:
                     e.dma_start(out=wsl[par], in_=d_wo[do]), "wsl%d" % par)
            plan.dma("sync", lambda e, do=do, par=par:
                     e.dma_start(out=st_res[par], in_=d_stream[do]), "sr%d" % par)
            plan.wait("vector", "wsl%d" % par)
            if do >= 2:
                plan.wait("vector", PE)
            plan.run("vector", lambda e, par=par: e.tensor_copy(wslb[par], wsl[par]),
                     inc=VE)
            ps_o = psum[:, par * 2048:(par + 1) * 2048]
            plan.wait("tensor", VE)
            if do >= 2:
                plan.wait("tensor", VE, evac_marks[do - 2])
            for fc in range(NDC):
                for ts in range(4):
                    sl = slice(ts * 512, (ts + 1) * 512)
                    plan.run("tensor", lambda e, par=par, fc=fc, sl=sl, ps_o=ps_o:
                             e.matmul(ps_o[:, sl], wslb[par][:, fc * 128:(fc + 1) * 128],
                                      yf[:, fc, sl], start=fc == 0, stop=fc == NDC - 1),
                             inc=(PE if (fc == NDC - 1 and ts == 3) else None))
            plan.wait("vector", PE)
            plan.wait("vector", "sr%d" % par)
            if do >= 2:
                plan.wait("vector", "so%d" % par)
            plan.run("vector", lambda e, par=par, ps_o=ps_o:
                     e.scalar_tensor_tensor(st_out[par], ps_o, 0.2, st_res[par],
                                            OP.mult, OP.add), inc=VE)
            evac_marks.append(plan.cnt[VE])
            plan.wait("sync", VE)
            plan.dma("sync", lambda e, do=do, par=par:
                     e.dma_start(out=d_stream[do], in_=st_out[par]), "so%d" % par)
        barrier()

    if stop_after == "prologue":
        blk = nc.Block("main")
        with blk:
            for deco, eng_name in ((blk.sync, "sync"), (blk.gpsimd, "gpsimd"),
                                   (blk.tensor, "tensor"), (blk.scalar, "scalar"),
                                   (blk.vector, "vector")):
                def mk(ops):
                    def body(e):
                        for f in ops:
                            f(e)
                    return body
                deco(mk(plan.ops[eng_name]))
        return nc
    emit_ln(d_x)
    if stop_after == "dumpln":
        dtile = W.view(0, [2048], F32)
        for dc in range(16):
            plan.run("vector", lambda e, dc=dc: e.tensor_copy(dtile, lnx[:, dc, :]),
                     inc=VE)
            plan.wait("gpsimd", VE)
            plan.dma("gpsimd", lambda e, dc=dc: e.dma_start(out=d_out[dc], in_=dtile),
                     "dmps")
            barrier()
        blk = nc.Block("main")
        with blk:
            for deco, eng_name in ((blk.sync, "sync"), (blk.gpsimd, "gpsimd"),
                                   (blk.tensor, "tensor"), (blk.scalar, "scalar"),
                                   (blk.vector, "vector")):
                def mk(ops):
                    def body(e):
                        for f in ops:
                            f(e)
                    return body
                deco(mk(plan.ops[eng_name]))
        return nc
    if stop_after == "ln":
        barrier()
        blk = nc.Block("main")
        with blk:
            for deco, eng_name in ((blk.sync, "sync"), (blk.gpsimd, "gpsimd"),
                                   (blk.tensor, "tensor"), (blk.scalar, "scalar"),
                                   (blk.vector, "vector")):
                def mk(ops):
                    def body(e):
                        for f in ops:
                            f(e)
                    return body
                deco(mk(plan.ops[eng_name]))
        return nc
    emit_ffn_a()
    if stop_after == "dumpln2":
        dtile2 = W.view(0, [2048], F32)
        for dc in range(16):
            plan.run("vector", lambda e, dc=dc: e.tensor_copy(dtile2, lnx[:, dc, :]),
                     inc=VE)
            plan.wait("gpsimd", VE)
            plan.dma("gpsimd", lambda e, dc=dc: e.dma_start(out=d_out[dc], in_=dtile2),
                     "dmps")
            barrier()
        blk = nc.Block("main")
        with blk:
            for deco, eng_name in ((blk.sync, "sync"), (blk.gpsimd, "gpsimd"),
                                   (blk.tensor, "tensor"), (blk.scalar, "scalar"),
                                   (blk.vector, "vector")):
                def mk(ops):
                    def body(e):
                        for f in ops:
                            f(e)
                    return body
                deco(mk(plan.ops[eng_name]))
        return nc
    if stop_after == "dumpfuse":
        btile = W.view(0, [4, 512], BF16)
        dtile = W.view(8192, [2048], F32)
        for i in range(16):
            plan.dma("sync", lambda e, i=i: e.dma_start(
                out=btile, in_=d_fuse[0][:, 4 * i:4 * i + 4, :]), "dmp")
            plan.wait("vector", "dmp")
            plan.run("vector", lambda e: e.tensor_copy(
                dtile, btile.rearrange("p a b -> p (a b)")), inc=VE)
            plan.wait("gpsimd", VE)
            plan.dma("gpsimd", lambda e, i=i: e.dma_start(out=d_out[i], in_=dtile),
                     "dmps")
            barrier()
        blk = nc.Block("main")
        with blk:
            for deco, eng_name in ((blk.sync, "sync"), (blk.gpsimd, "gpsimd"),
                                   (blk.tensor, "tensor"), (blk.scalar, "scalar"),
                                   (blk.vector, "vector")):
                def mk(ops):
                    def body(e):
                        for f in ops:
                            f(e)
                    return body
                deco(mk(plan.ops[eng_name]))
        return nc
    if stop_after == "ffna2":
        emit_ffn_a()
        barrier()
        blk = nc.Block("main")
        with blk:
            for deco, eng_name in ((blk.sync, "sync"), (blk.gpsimd, "gpsimd"),
                                   (blk.tensor, "tensor"), (blk.scalar, "scalar"),
                                   (blk.vector, "vector")):
                def mk(ops):
                    def body(e):
                        for f in ops:
                            f(e)
                    return body
                deco(mk(plan.ops[eng_name]))
        return nc
    if stop_after == "ffna":
        barrier()
        blk = nc.Block("main")
        with blk:
            for deco, eng_name in ((blk.sync, "sync"), (blk.gpsimd, "gpsimd"),
                                   (blk.tensor, "tensor"), (blk.scalar, "scalar"),
                                   (blk.vector, "vector")):
                def mk(ops):
                    def body(e):
                        for f in ops:
                            f(e)
                    return body
                deco(mk(plan.ops[eng_name]))
        return nc
    if stop_after.startswith("ffnb:"):
        _, a, b = stop_after.split(":")
        emit_ffn_b(d_x, d_out, int(a), int(b))
    elif stop_after == "ffn1":
        emit_ffn_b(d_x, d_out)
    else:
        emit_ffn_b(d_x, d_stream)
        emit_ln(d_stream)
        emit_qkv()
        emit_attention()
        emit_oproj()
        if stop_after == "attn":
            tmp = W.view(0, [2048], F32)
            for i in range(16):
                plan.dma("sync", lambda e, i=i: e.dma_start(out=tmp, in_=d_stream[i]),
                         "dmp")
                plan.wait("gpsimd", "dmp")
                plan.dma("gpsimd", lambda e, i=i: e.dma_start(out=d_out[i], in_=tmp),
                         "dmps")
                barrier()
        else:
            emit_ln(d_stream)
            emit_ffn_a()
            emit_ffn_b(d_stream, d_out)

    barrier()

    blk = nc.Block("main")
    with blk:
        for deco, eng_name in ((blk.sync, "sync"), (blk.gpsimd, "gpsimd"),
                               (blk.tensor, "tensor"), (blk.scalar, "scalar"),
                               (blk.vector, "vector")):
            def make_body(ops):
                def body(e):
                    for f in ops:
                        f(e)
                return body
            deco(make_body(plan.ops[eng_name]))
    return nc


_NC_CACHE = {}


def get_nc(stop_after="full"):
    if stop_after not in _NC_CACHE:
        _NC_CACHE[stop_after] = build_program(stop_after)
    return _NC_CACHE[stop_after]


def kernel(**inputs):
    in_maps = stage_inputs(inputs)
    nc = get_nc("full")
    res = run_bass_kernel_spmd(nc, in_maps, core_ids=list(range(8)))
    return assemble_output(res.results)


# version 08 (final)
# speedup vs baseline: 1.1774x; 1.1774x over previous
import numpy as np
import ml_dtypes
import concourse.bass as bass
import concourse.mybir as mybir
from concourse.bass_utils import run_bass_kernel_spmd

F32 = mybir.dt.float32
BF16 = mybir.dt.bfloat16
I8 = mybir.dt.int8
U8 = mybir.dt.uint8
AF = mybir.ActivationFunctionType
OP = mybir.AluOpType
AX = mybir.AxisListType

P = 128
D = 2048
NDC = 16
INTER = 8192
NIC = 64
NH = 8
NB = 4
T = 4096
NG = 8
KVROWS = 4224
LAMBDA_INIT = 0.8
RMS_EPS = 1.1920929e-07
LN_EPS = 1e-5
ISQRT_HD = float(1.0 / np.sqrt(128.0))
EXP_SHIFT = -12.0

ENGINES = ("sync", "gpsimd", "tensor", "scalar", "vector")
LD, ST, CC = "ld", "st", "cc"
PE, SC, VE, GP = "pe", "sc", "ve", "gp"



def own_blocks(role):
    out = []
    for k in range(8):
        out += [4 * k, 4 * k + 3] if role == 0 else [4 * k + 1, 4 * k + 2]
    return out


def owner_and_local(j):
    m = j % 4
    if m in (0, 3):
        return 0, 2 * (j // 4) + (0 if m == 0 else 1)
    return 1, 2 * (j // 4) + (0 if m == 1 else 1)


def build_masks(role):
    blocks = own_blocks(role)
    m = np.zeros((NG, P, 4, 512), dtype=np.float32)
    for g in range(NG):
        qglob = np.empty(256, dtype=np.int64)
        qglob[:128] = blocks[2 * g] * 128 + np.arange(128)
        qglob[128:] = blocks[2 * g + 1] * 128 + np.arange(128)
        for j in range(4):
            key = (4 * g + j) * 128 + np.arange(128)
            keep = (qglob[None, :] >= key[:, None]).astype(np.float32)
            m[g, :, j, :256] = keep
            m[g, :, j, 256:] = keep
    return np.ascontiguousarray(m.astype(ml_dtypes.bfloat16))


def stage_inputs(inputs):
    x = np.asarray(inputs["L0_attn_x"], dtype=np.float32)
    wq = np.asarray(inputs["L0_attn_wq"], dtype=np.float32)
    wk = np.asarray(inputs["L0_attn_wk"], dtype=np.float32)
    wv = np.asarray(inputs["L0_attn_wv"], dtype=np.float32)
    wo = np.asarray(inputs["L0_attn_wo"], dtype=np.float32)
    wg = np.asarray(inputs["L0_ffn_w_gate"])
    wu = np.asarray(inputs["L0_ffn_w_up"])
    wd = np.asarray(inputs["L0_ffn_w_down"])
    sg = np.asarray(inputs["L0_ffn_s_gate"], dtype=np.float32)
    su = np.asarray(inputs["L0_ffn_s_up"], dtype=np.float32)
    sd = np.asarray(inputs["L0_ffn_s_down"], dtype=np.float32)

    def stage_ofmajor(w):
        a = w.reshape(16, 128, 16, 128)
        return np.ascontiguousarray(a.transpose(0, 3, 2, 1).reshape(16, 128, 2048))

    def stage_gu(w):
        a = w.reshape(16, 128, 64, 128)
        return np.ascontiguousarray(a.transpose(2, 1, 0, 3).reshape(64, 128, 2048))

    def stage_down(w):
        a = w.reshape(64, 128, 16, 128)
        return np.ascontiguousarray(a.transpose(2, 1, 0, 3).reshape(16, 128, 8192))

    wq_s = stage_ofmajor(wq)
    wk_s = stage_ofmajor(wk)
    wo_s = stage_ofmajor(wo)
    wv_s = np.ascontiguousarray(wv.T.reshape(16, 128, 2048))
    wg_s = stage_gu(wg)
    wu_s = stage_gu(wu)
    wd_s = stage_down(wd)
    sg_s = np.ascontiguousarray(sg.reshape(64, 128).T)
    su_s = np.ascontiguousarray(su.reshape(64, 128).T)
    sd_s = np.ascontiguousarray(sd.reshape(16, 128).T)
    lam_s = np.ascontiguousarray(np.concatenate([
        np.asarray(inputs["L0_attn_lq1"], dtype=np.float32),
        np.asarray(inputs["L0_attn_lk1"], dtype=np.float32),
        np.asarray(inputs["L0_attn_lq2"], dtype=np.float32),
        np.asarray(inputs["L0_attn_lk2"], dtype=np.float32),
    ]).reshape(1, 1024))
    masks = [build_masks(0), build_masks(1)]

    in_maps = []
    for c in range(8):
        b, r = c // 2, c % 2
        rows = np.concatenate([np.arange(j * 128, (j + 1) * 128) for j in own_blocks(r)])
        x_own = np.ascontiguousarray(x[b][rows].T).reshape(16, 128, 2048)
        in_maps.append({
            "x": x_own,
            "wq": wq_s, "wk": wk_s, "wv": wv_s, "wo": wo_s,
            "wg": wg_s, "wu": wu_s, "wd": wd_s,
            "sg": sg_s, "su": su_s, "sd": sd_s,
            "lam": lam_s, "mask": masks[r],
        })
    return in_maps


def assemble_output(results):
    out = np.empty((NB, T, D), dtype=np.float32)
    for c in range(8):
        b, r = c // 2, c % 2
        rows = np.concatenate([np.arange(j * 128, (j + 1) * 128) for j in own_blocks(r)])
        out[b][rows] = results[c]["out"].reshape(2048, 2048).T
    return out



class Plan:
    def __init__(self, nc, mark_all_pe=False):
        self.mark_all_pe = mark_all_pe
        self.nc = nc
        self.ops = {e: [] for e in ENGINES}
        self.sems = {}
        self.cnt = {}
        self.dma_streams = {}

    def sem(self, name):
        if name not in self.sems:
            self.sems[name] = self.nc.alloc_semaphore(name)
            self.cnt[name] = 0
        return self.sems[name]

    def dma(self, eng, fn, stream):
        self.dma_streams.setdefault(stream, eng)
        return self.run(eng, fn, inc=stream, amt=16, dma=True)

    def wait(self, eng, name, val=None):
        s = self.sem(name)
        v = self.cnt[name] if val is None else val
        if v <= 0:
            return
        self.ops[eng].append(lambda e, s=s, v=v: e.wait_ge(s, v))

    CHAIN = {"vector": VE, "scalar": SC, "gpsimd": GP}

    def run(self, eng, fn, inc=None, amt=1, dma=False):
        if (eng == "tensor" and self.mark_all_pe and not dma and inc is None):
            inc = PE
        ch = None if dma else self.CHAIN.get(eng)
        if ch is not None:
            assert inc is None or inc == ch, (eng, inc)
            s = self.sem(ch)
            prev = self.cnt[ch]
            self.cnt[ch] += 1
            def op(e, fn=fn, s=s, prev=prev):
                if prev > 0:
                    e.wait_ge(s, prev)
                fn(e).then_inc(s, 1)
            self.ops[eng].append(op)
            return self.cnt[ch]
        if inc is not None:
            s = self.sem(inc)
            self.cnt[inc] += amt
            self.ops[eng].append(lambda e, fn=fn, s=s, a=amt: fn(e).then_inc(s, a))
            return self.cnt[inc]
        self.ops[eng].append(fn)
        return None

    def barrier(self, flushers):
        for stream, eng in self.dma_streams.items():
            self.wait(eng, stream)
        if self.cnt.get(CC, 0):
            self.wait("gpsimd", CC)
        s = self.sem("bar")
        self.wait("tensor", PE)
        for eng in ENGINES:
            ch = self.CHAIN.get(eng)
            if ch is not None:
                self.wait(eng, ch)
            self.cnt["bar"] += 1
            f = flushers.get(eng)
            if f is None:
                self.ops[eng].append(lambda e, s=s: e.nop().then_inc(s, 1))
            else:
                self.ops[eng].append(lambda e, s=s, f=f: f(e).then_inc(s, 1))
        tgt = self.cnt["bar"]
        for eng in ENGINES:
            self.ops[eng].append(lambda e, s=s, t=tgt: e.wait_ge(s, t))


class Carver:
    def __init__(self, nc, name, nbytes):
        self.t = nc.alloc_sbuf_tensor(name, [P, nbytes], U8)
        self.nbytes = nbytes

    def view(self, off, free_shape, dtype):
        n = int(np.prod(free_shape)) * mybir.dt.size(dtype)
        assert off + n <= self.nbytes, (off, n, self.nbytes)
        v = self.t[:, off:off + n].bitcast(dtype)
        if len(free_shape) == 1:
            return v
        if len(free_shape) == 2:
            return v.rearrange("p (a b) -> p a b", a=free_shape[0])
        if len(free_shape) == 3:
            return v.rearrange("p (a b c) -> p a b c", a=free_shape[0], b=free_shape[1])
        raise ValueError(free_shape)



FFNA_PREMM = 0
FFNA_MODE = "normal"
FFNB_LIGHT = set()


def build_program(stop_after="full", mark_all_pe=False):
    nc = bass.Bass("TRN2", target_bir_lowering=False, debug=False)

    d_x = nc.dram_tensor("x", [16, 128, 2048], F32, kind="ExternalInput")
    d_wq = nc.dram_tensor("wq", [16, 128, 2048], F32, kind="ExternalInput")
    d_wk = nc.dram_tensor("wk", [16, 128, 2048], F32, kind="ExternalInput")
    d_wv = nc.dram_tensor("wv", [16, 128, 2048], F32, kind="ExternalInput")
    d_wo = nc.dram_tensor("wo", [16, 128, 2048], F32, kind="ExternalInput")
    d_wg = nc.dram_tensor("wg", [64, 128, 2048], I8, kind="ExternalInput")
    d_wu = nc.dram_tensor("wu", [64, 128, 2048], I8, kind="ExternalInput")
    d_wd = nc.dram_tensor("wd", [16, 128, 8192], I8, kind="ExternalInput")
    d_sg = nc.dram_tensor("sg", [128, 64], F32, kind="ExternalInput")
    d_su = nc.dram_tensor("su", [128, 64], F32, kind="ExternalInput")
    d_sd = nc.dram_tensor("sd", [128, 16], F32, kind="ExternalInput")
    d_lam = nc.dram_tensor("lam", [1, 1024], F32, kind="ExternalInput")
    d_mask = nc.dram_tensor("mask", [8, 128, 4, 512], BF16, kind="ExternalInput")
    d_out = nc.dram_tensor("out", [16, 128, 2048], F32, kind="ExternalOutput")

    d_stream = nc.dram_tensor("stream", [16, 128, 2048], F32)
    d_fuse = nc.dram_tensor("fuse", [4, 128, 64, 512], BF16)
    d_kvst = nc.dram_tensor("kvst", [KVROWS, 2048], BF16)
    d_kvg = nc.dram_tensor("kvg", [2 * KVROWS, 2048], BF16)
    d_qst = nc.dram_tensor("qst", [2048, 2048], BF16)
    d_rq = nc.dram_tensor("rq", [16, 2048], BF16)
    d_rows = nc.dram_tensor("rows", [2, 2048], F32)
    d_finrows = nc.dram_tensor("finrows", [2, 512], BF16)

    A = Carver(nc, "arenaA", 65536)
    W = Carver(nc, "arenaW", 81920)
    lnx = A.view(0, [16, 2048], BF16)
    fuse_sl = A.view(0, [64, 512], BF16)
    yf = A.view(0, [16, 2048], BF16)

    ones_bf = nc.alloc_sbuf_tensor("ones_bf", [P, 1], BF16)
    sexp_col = nc.alloc_sbuf_tensor("sexp_col", [P, 1], F32)
    bexp_col = nc.alloc_sbuf_tensor("bexp_col", [P, 1], F32)
    m05 = nc.alloc_sbuf_tensor("m05", [1, 1], F32)
    lamv = nc.alloc_sbuf_tensor("lamv", [1, 1], F32)
    flushv = nc.alloc_sbuf_tensor("flushv", [1, 4], F32)
    sg_sb = nc.alloc_sbuf_tensor("sg_sb", [P, 64], F32)
    th_sb = nc.alloc_sbuf_tensor("th_sb", [P, 64], F32)
    su_sb = nc.alloc_sbuf_tensor("su_sb", [P, 64], F32)
    sd05 = nc.alloc_sbuf_tensor("sd05", [P, 16], F32)

    psum = nc.alloc_psum_tensor("psum", [P, 4096], F32)

    plan = Plan(nc, mark_all_pe=mark_all_pe)
    FLUSH = {
        "vector": lambda e: e.memset(flushv[0:1, 0:1], 0.0),
        "gpsimd": lambda e: e.memset(flushv[0:1, 1:2], 0.0),
        "scalar": lambda e: e.memzero(flushv[0:1, 2:3]),
    }

    def barrier():
        plan.barrier(FLUSH)

    lam_sb = W.view(0, [1024], F32)
    lrow = W.view(4096, [256], F32)
    lrow2 = W.view(5120, [256], F32)
    lr1 = W.view(6144, [1], F32)
    lr2 = W.view(6176, [1], F32)

    plan.dma("sync", lambda e: e.dma_start(out=sg_sb[:, :], in_=d_sg[:, :]), "pld")
    plan.dma("sync", lambda e: e.dma_start(out=su_sb[:, :], in_=d_su[:, :]), "pld")
    plan.dma("sync", lambda e: e.dma_start(out=sd05[:, :], in_=d_sd[:, :]), "pld")
    plan.dma("sync", lambda e: e.dma_start(out=lam_sb[0:1, :], in_=d_lam[:, :]), "pld")
    plan.wait("vector", "pld")
    plan.run("vector", lambda e: e.memset(ones_bf[:, :], 1.0))
    plan.run("vector", lambda e: e.memset(sexp_col[:, :], ISQRT_HD))
    plan.run("vector", lambda e: e.memset(bexp_col[:, :], EXP_SHIFT))
    plan.run("vector", lambda e: e.memset(m05[:, :], -0.5))
    plan.run("vector", lambda e: e.tensor_scalar_mul(sd05[:, :], sd05[:, :], 0.5))
    plan.run("vector", lambda e: e.reciprocal(th_sb[:, :], sg_sb[:, :]))
    plan.run("vector", lambda e: e.tensor_scalar_mul(th_sb[:, :], th_sb[:, :], -45.0))
    plan.run("vector", lambda e: e.tensor_mul(lrow[0:1, :], lam_sb[0:1, 0:256], lam_sb[0:1, 256:512]))
    plan.run("vector", lambda e: e.tensor_mul(lrow2[0:1, :], lam_sb[0:1, 512:768], lam_sb[0:1, 768:1024]))
    plan.run("vector", lambda e: e.reduce_sum(lr1[0:1, :], lrow[0:1, :], axis=AX.X))
    plan.run("vector", lambda e: e.reduce_sum(lr2[0:1, :], lrow2[0:1, :], axis=AX.X), inc=VE)
    plan.wait("scalar", VE)
    plan.run("scalar", lambda e: e.activation(lr1[0:1, :], lr1[0:1, :], AF.Exp))
    plan.run("scalar", lambda e: e.activation(lr2[0:1, :], lr2[0:1, :], AF.Exp), inc=SC)
    plan.wait("vector", SC)
    plan.run("vector", lambda e: e.tensor_sub(lamv[0:1, :], lr1[0:1, :], lr2[0:1, :]))
    plan.run("vector", lambda e: e.tensor_scalar_add(lamv[0:1, :], lamv[0:1, :], LAMBDA_INIT))
    zpad = W.view(8192, [2048], BF16)
    plan.run("vector", lambda e: e.memset(zpad, 0.0), inc=VE)
    plan.wait("gpsimd", VE)
    plan.dma("gpsimd", lambda e: e.dma_start(out=d_kvst.ap()[4112:4224, :],
                                             in_=zpad[0:112, :]), "pld2")
    barrier()

    def emit_ln(src):
        t_x = [W.view(0, [2048], F32), W.view(8192, [2048], F32)]
        t_xb = [W.view(16384, [2048], BF16), W.view(20480, [2048], BF16)]
        t_sq = [W.view(24576, [2048], BF16), W.view(28672, [2048], BF16)]
        r_a = W.view(32768, [2048], F32)
        r_b = W.view(40960, [2048], F32)
        r_c = W.view(49152, [2048], F32)
        bc_mean = W.view(57344, [2048], F32)
        bc_rstd = W.view(65536, [2048], F32)
        ps_sum = psum[0:1, 0:2048]
        ps_sq = psum[0:1, 2048:4096]

        for dc in range(NDC):
            par = dc % 2
            if dc >= 2:
                plan.wait("sync", VE)
                plan.wait("sync", SC)
            plan.dma("sync", lambda e, dc=dc, par=par:
                     e.dma_start(out=t_x[par], in_=src[dc]), "tx%d" % par)
            plan.wait("vector", "tx%d" % par)
            if dc >= 2:
                plan.wait("vector", PE)
            plan.run("vector", lambda e, par=par: e.tensor_copy(t_xb[par], t_x[par]), inc=VE)
            plan.wait("scalar", "tx%d" % par)
            if dc >= 2:
                plan.wait("scalar", PE)
            plan.run("scalar", lambda e, par=par:
                     e.activation(t_sq[par], t_x[par], AF.Square), inc=SC)
            plan.wait("tensor", VE)
            plan.wait("tensor", SC)
            first, last = dc == 0, dc == NDC - 1
            for half in range(4):
                sl = slice(half * 512, (half + 1) * 512)
                plan.run("tensor", lambda e, par=par, sl=sl, first=first, last=last:
                         e.matmul(ps_sum[:, sl], ones_bf[:, :], t_xb[par][:, sl],
                                  start=first, stop=last))
            for half in range(4):
                sl = slice(half * 512, (half + 1) * 512)
                plan.run("tensor", lambda e, par=par, sl=sl, first=first, last=last:
                         e.matmul(ps_sq[:, sl], ones_bf[:, :], t_sq[par][:, sl],
                                  start=first, stop=last),
                         inc=(PE if (last and half == 3) else None))
        plan.wait("vector", PE)
        plan.run("vector", lambda e: e.tensor_scalar_mul(r_a[0:1, :], ps_sum, 1.0 / D))
        plan.run("vector", lambda e: e.tensor_scalar_mul(r_b[0:1, :], ps_sq, 1.0 / D))
        plan.run("vector", lambda e: e.tensor_mul(r_c[0:1, :], r_a[0:1, :], r_a[0:1, :]))
        plan.run("vector", lambda e: e.tensor_sub(r_b[0:1, :], r_b[0:1, :], r_c[0:1, :]))
        plan.run("vector", lambda e: e.tensor_scalar_add(r_b[0:1, :], r_b[0:1, :], LN_EPS),
                 inc=VE)
        plan.wait("scalar", VE)
        plan.run("scalar", lambda e: e.activation(r_c[0:1, :], r_b[0:1, :], AF.Ln))
        plan.run("scalar", lambda e: e.activation(r_b[0:1, :], r_c[0:1, :], AF.Exp,
                                                  scale=m05[0:1, :]), inc=SC)
        plan.wait("gpsimd", SC)
        plan.dma("gpsimd", lambda e: e.dma_start(out=d_rows[0:1, :], in_=r_a[0:1, :]),
                 "lnrow")
        plan.dma("gpsimd", lambda e: e.dma_start(out=d_rows[1:2, :], in_=r_b[0:1, :]),
                 "lnrow")
        plan.wait("gpsimd", "lnrow")
        plan.dma("gpsimd", lambda e: e.dma_start(
            out=bc_mean, in_=d_rows[0:1, :].broadcast_to([P, 2048])), "lnbc")
        plan.dma("gpsimd", lambda e: e.dma_start(
            out=bc_rstd, in_=d_rows[1:2, :].broadcast_to([P, 2048])), "lnbc")
        for dc in range(NDC):
            par = dc % 2
            plan.wait("sync", VE)
            if dc < 2:
                plan.wait("sync", SC)
            plan.dma("sync", lambda e, dc=dc, par=par:
                     e.dma_start(out=t_x[par], in_=src[dc]), "tx%d" % par)
            plan.wait("vector", "tx%d" % par)
            plan.wait("vector", "lnbc")
            plan.run("vector", lambda e, par=par: e.tensor_sub(t_x[par], t_x[par], bc_mean))
            plan.run("vector", lambda e, dc=dc, par=par:
                     e.tensor_mul(lnx[:, dc, :], t_x[par], bc_rstd), inc=VE)
        barrier()

    def emit_ffn_a():
        w8_g = [W.view(0, [2048], I8), W.view(2048, [2048], I8)]
        w8_u = [W.view(4096, [2048], I8), W.view(6144, [2048], I8)]
        wbf_g = [W.view(57344, [2048], BF16), W.view(61440, [2048], BF16),
                 W.view(65536, [2048], BF16), W.view(69632, [2048], BF16)]
        wbf_u = [W.view(16384, [2048], BF16), W.view(20480, [2048], BF16),
                 W.view(73728, [2048], BF16), W.view(77824, [2048], BF16)]
        g_act = [W.view(24576, [2048], BF16), W.view(28672, [2048], BF16)]
        t_gate = [W.view(40960, [2048], F32), W.view(49152, [2048], F32)]
        fuse_bf = [W.view(32768, [2048], BF16), W.view(36864, [2048], BF16)]
        ps_g = psum[:, 0:2048]
        ps_u = psum[:, 2048:4096]

        for i in range(FFNA_PREMM):
            plan.run("tensor", lambda e: e.matmul(
                psum[:, 0:512], lnx[:, 0, 0:128],
                lnx[:, 1, 0:512], start=True, stop=True))
        up_marks = []
        pending_stores = None
        for ic in range(NIC):
            par = ic % 2
            par4 = ic % 4
            if ic >= 2:
                plan.wait("sync", GP)
            plan.dma("sync", lambda e, ic=ic, par=par:
                     e.dma_start(out=w8_g[par], in_=d_wg[ic]), "w8%d" % par)
            plan.dma("sync", lambda e, ic=ic, par=par:
                     e.dma_start(out=w8_u[par], in_=d_wu[ic]), "w8%d" % par)
            plan.wait("gpsimd", "w8%d" % par)
            if ic >= 4:
                plan.wait("gpsimd", PE, up_marks[ic - 4])
            plan.run("gpsimd", lambda e, par=par, par4=par4:
                     e.tensor_copy(wbf_g[par4], w8_g[par]), inc=GP)
            plan.run("gpsimd", lambda e, par=par, par4=par4:
                     e.tensor_copy(wbf_u[par4], w8_u[par]), inc=GP)
            if pending_stores is not None:
                pending_stores()
                pending_stores = None
            plan.wait("tensor", GP)
            if ic >= 1:
                plan.wait("tensor", VE)
            for dc in range(NDC):
                for ts in range(4):
                    sl = slice(ts * 512, (ts + 1) * 512)
                    plan.run("tensor", lambda e, par4=par4, dc=dc, sl=sl:
                             e.matmul(ps_g[:, sl], wbf_g[par4][:, dc * 128:(dc + 1) * 128],
                                      lnx[:, dc, sl], start=dc == 0, stop=dc == NDC - 1),
                             inc=(PE if (dc == NDC - 1 and ts == 3) else None))
            gate_mark = plan.cnt[PE]
            if ic >= 1:
                plan.wait("tensor", VE)
            for dc in range(NDC):
                for ts in range(4):
                    sl = slice(ts * 512, (ts + 1) * 512)
                    plan.run("tensor", lambda e, par4=par4, dc=dc, sl=sl:
                             e.matmul(ps_u[:, sl], wbf_u[par4][:, dc * 128:(dc + 1) * 128],
                                      lnx[:, dc, sl], start=dc == 0, stop=dc == NDC - 1),
                             inc=(PE if (dc == NDC - 1 and ts == 3) else None))
            up_marks.append(plan.cnt[PE])
            plan.wait("vector", PE, gate_mark)
            if ic >= 2:
                plan.wait("vector", SC)
            plan.run("vector", lambda e, ic=ic, par=par:
                     e.tensor_scalar(t_gate[par], ps_g, th_sb[:, ic:ic + 1],
                                     sg_sb[:, ic:ic + 1], OP.max, OP.mult), inc=VE)
            clamp_mark = plan.cnt[VE]
            plan.wait("scalar", VE, clamp_mark)
            if ic >= 2:
                plan.wait("scalar", VE)
            plan.run("scalar", lambda e, par=par:
                     e.activation(g_act[par], t_gate[par], AF.Silu), inc=SC)
            plan.wait("vector", PE)
            plan.wait("vector", SC)
            if ic >= 2:
                plan.wait("vector", "fb%d" % par)
            if FFNA_MODE == "graw":
                plan.run("vector", lambda e, ic=ic, par=par:
                         e.tensor_copy(fuse_bf[par], t_gate[par]), inc=VE)
            elif FFNA_MODE == "gact":
                plan.run("vector", lambda e, ic=ic, par=par:
                         e.tensor_copy(fuse_bf[par], g_act[par]), inc=VE)
            elif FFNA_MODE == "up":
                plan.run("vector", lambda e, ic=ic, par=par:
                         e.tensor_scalar(fuse_bf[par], ps_u, su_sb[:, ic:ic + 1], None,
                                         OP.mult), inc=VE)
            else:
                plan.run("vector", lambda e, ic=ic, par=par:
                         e.scalar_tensor_tensor(fuse_bf[par], ps_u, su_sb[:, ic:ic + 1],
                                                g_act[par], OP.mult, OP.mult), inc=VE)
            fuse_mark = plan.cnt[VE]

            def mk_stores(ic=ic, par=par, fuse_mark=fuse_mark):
                def emit():
                    plan.wait("gpsimd", VE, fuse_mark)
                    for tb in range(4):
                        plan.dma("gpsimd", lambda e, ic=ic, par=par, tb=tb:
                                 e.dma_start(out=d_fuse[tb][:, ic, :],
                                             in_=fuse_bf[par][:, tb * 512:(tb + 1) * 512]),
                                 "fb%d" % par)
                return emit
            pending_stores = mk_stores()
        pending_stores()
        barrier()

    def emit_ffn_b(res_src, dst, tb_lim=4, do_lim=16):
        wd8 = [W.view(0, [8192], I8), W.view(8192, [8192], I8)]
        wdbf = [W.view(16384, [8192], BF16), W.view(32768, [8192], BF16)]
        t_res = [W.view(49152, [512], F32), W.view(51200, [512], F32)]
        t_out = [W.view(53248, [512], F32), W.view(55296, [512], F32)]
        tb_evac_mark = 0
        for tb in range(tb_lim):
            if tb >= 1:
                plan.wait("sync", VE, tb_evac_mark)
            for q in range(4):
                if "fsl" in FFNB_LIGHT:
                    plan.dma("sync", lambda e, tb=tb, q=q: e.dma_start(
                        out=fuse_sl[:, q * 16:(q + 1) * 16, 0:16],
                        in_=d_fuse[tb][:, q * 16:(q + 1) * 16, 0:16]), "fsl")
                else:
                    plan.dma("sync", lambda e, tb=tb, q=q: e.dma_start(
                        out=fuse_sl[:, q * 16:(q + 1) * 16, :],
                        in_=d_fuse[tb][:, q * 16:(q + 1) * 16, :]), "fsl")
            for do in range(do_lim):
                par = do % 2
                idx = tb * NDC + do
                if idx >= 2:
                    plan.wait("sync", GP)
                    plan.wait("sync", VE)
                if "wd" in FFNB_LIGHT:
                    plan.dma("sync", lambda e, do=do, par=par:
                             e.dma_start(out=wd8[par][:, 0:128], in_=d_wd[do][:, 0:128]),
                             "wd8%d" % par)
                else:
                    plan.dma("sync", lambda e, do=do, par=par:
                             e.dma_start(out=wd8[par], in_=d_wd[do]), "wd8%d" % par)
                if "res" in FFNB_LIGHT:
                    plan.dma("sync", lambda e, do=do, tb=tb, par=par:
                             e.dma_start(out=t_res[par][:, 0:16],
                                         in_=res_src[do][:, 0:16]), "tr%d" % par)
                else:
                    plan.dma("sync", lambda e, do=do, tb=tb, par=par:
                             e.dma_start(out=t_res[par],
                                         in_=res_src[do][:, tb * 512:(tb + 1) * 512]),
                             "tr%d" % par)
                plan.wait("gpsimd", "wd8%d" % par)
                if idx >= 2:
                    plan.wait("gpsimd", PE)
                if "conv" in FFNB_LIGHT:
                    plan.run("gpsimd", lambda e, par=par:
                             e.tensor_copy(wdbf[par][:, 0:16], wd8[par][:, 0:16]), inc=GP)
                else:
                    plan.run("gpsimd", lambda e, par=par:
                             e.tensor_copy(wdbf[par][:, 0:4096], wd8[par][:, 0:4096]),
                             inc=GP)
                plan.wait("vector", "wd8%d" % par)
                if idx >= 2:
                    plan.wait("vector", PE)
                if "conv" in FFNB_LIGHT:
                    plan.run("vector", lambda e, par=par:
                             e.tensor_copy(wdbf[par][:, 4096:4112], wd8[par][:, 4096:4112]),
                             inc=VE)
                else:
                    plan.run("vector", lambda e, par=par:
                             e.tensor_copy(wdbf[par][:, 4096:8192], wd8[par][:, 4096:8192]),
                             inc=VE)
                conv_ve_mark = plan.cnt[VE]
                ps_d = psum[:, par * 512:(par + 1) * 512]
                plan.wait("tensor", GP)
                plan.wait("tensor", VE, conv_ve_mark)
                plan.wait("tensor", "fsl")
                for ich in range(NIC):
                    if "mm" in FFNB_LIGHT:
                        plan.run("tensor", lambda e, par=par, ich=ich, ps_d=ps_d:
                                 e.matmul(ps_d[:, 0:16],
                                          wdbf[par][:, ich * 128:(ich + 1) * 128],
                                          fuse_sl[:, ich, 0:16], start=ich == 0,
                                          stop=ich == NIC - 1),
                                 inc=(PE if ich == NIC - 1 else None))
                    else:
                        plan.run("tensor", lambda e, par=par, ich=ich, ps_d=ps_d:
                                 e.matmul(ps_d, wdbf[par][:, ich * 128:(ich + 1) * 128],
                                          fuse_sl[:, ich, :], start=ich == 0,
                                          stop=ich == NIC - 1),
                                 inc=(PE if ich == NIC - 1 else None))
                plan.wait("vector", PE)
                plan.wait("vector", "tr%d" % par)
                if idx >= 2:
                    plan.wait("vector", "to%d" % par)
                if "evac" in FFNB_LIGHT:
                    plan.run("vector", lambda e, do=do, par=par, ps_d=ps_d:
                             e.scalar_tensor_tensor(t_out[par][:, 0:16], ps_d[:, 0:16],
                                                    sd05[:, do:do + 1],
                                                    t_res[par][:, 0:16],
                                                    OP.mult, OP.add), inc=VE)
                else:
                    plan.run("vector", lambda e, do=do, par=par, ps_d=ps_d:
                             e.scalar_tensor_tensor(t_out[par], ps_d, sd05[:, do:do + 1],
                                                    t_res[par], OP.mult, OP.add), inc=VE)
                plan.wait("gpsimd", VE)
                plan.dma("gpsimd", lambda e, do=do, tb=tb, par=par:
                         e.dma_start(out=dst[do][:, tb * 512:(tb + 1) * 512],
                                     in_=t_out[par]), "to%d" % par)
                if do == do_lim - 1:
                    tb_evac_mark = plan.cnt[VE]
        barrier()

    def emit_proj_qk(w_dram, stash_dst_fn, rrow_dst_fn):
        wsl = [W.view(0, [2048], F32), W.view(8192, [2048], F32)]
        wslb = [W.view(16384, [2048], BF16), W.view(20480, [2048], BF16)]
        sqb = [W.view(24576, [2048], BF16), W.view(28672, [2048], BF16)]
        outb = [W.view(32768, [2048], BF16), W.view(36864, [2048], BF16)]
        r_a = W.view(40960, [2048], F32)
        r_c = W.view(49152, [2048], F32)
        rowbf = W.view(57344, [2048], BF16)
        ps_q = psum[:, 0:2048]
        ps_rr = psum[0:1, 2048:4096]

        for of in range(NDC):
            par = of % 2
            plan.wait("sync", VE)
            plan.dma("sync", lambda e, of=of, par=par:
                     e.dma_start(out=wsl[par], in_=w_dram[of]), "wsl%d" % par)
            plan.wait("vector", "wsl%d" % par)
            if of >= 2:
                plan.wait("vector", PE)
            plan.run("vector", lambda e, par=par: e.tensor_copy(wslb[par], wsl[par]),
                     inc=VE)
            plan.wait("tensor", VE)
            plan.wait("tensor", SC)
            for dc in range(NDC):
                for ts in range(4):
                    sl = slice(ts * 512, (ts + 1) * 512)
                    plan.run("tensor", lambda e, par=par, dc=dc, sl=sl:
                             e.matmul(ps_q[:, sl], wslb[par][:, dc * 128:(dc + 1) * 128],
                                      lnx[:, dc, sl], start=dc == 0, stop=dc == NDC - 1),
                             inc=(PE if (dc == NDC - 1 and ts == 3) else None))
            plan.wait("scalar", PE)
            plan.wait("scalar", "ob%d" % par)
            plan.run("scalar", lambda e, par=par:
                     e.activation(sqb[par], ps_q, AF.Square))
            plan.run("scalar", lambda e, par=par:
                     e.activation(outb[par], ps_q, AF.Copy), inc=SC)
            plan.wait("tensor", SC)
            plan.wait("tensor", VE)
            for half in range(4):
                sl = slice(half * 512, (half + 1) * 512)
                plan.run("tensor", lambda e, par=par, sl=sl:
                         e.matmul(ps_rr[:, sl], ones_bf[:, :], sqb[par][:, sl],
                                  start=True, stop=True),
                         inc=(PE if half == 3 else None))
            plan.wait("vector", PE)
            plan.run("vector", lambda e: e.tensor_scalar_mul(r_a[0:1, :], ps_rr, 1.0 / 128.0))
            plan.run("vector", lambda e: e.tensor_scalar_add(r_a[0:1, :], r_a[0:1, :],
                                                             RMS_EPS), inc=VE)
            plan.wait("scalar", VE)
            plan.run("scalar", lambda e: e.activation(r_c[0:1, :], r_a[0:1, :], AF.Ln))
            plan.run("scalar", lambda e: e.activation(r_a[0:1, :], r_c[0:1, :], AF.Exp,
                                                      scale=m05[0:1, :]), inc=SC)
            plan.wait("vector", SC)
            plan.wait("vector", "rrow")
            plan.run("vector", lambda e: e.tensor_copy(rowbf[0:1, :], r_a[0:1, :]), inc=VE)
            plan.wait("gpsimd", VE)
            plan.wait("gpsimd", SC)
            plan.dma("gpsimd", lambda e, of=of:
                     e.dma_start(out=rrow_dst_fn(of), in_=rowbf[0:1, :]), "rrow")
            plan.dma("gpsimd", lambda e, of=of, par=par:
                     e.dma_start(out=stash_dst_fn(of), in_=outb[par]), "ob%d" % par)

    def emit_vproj():
        wsl = [W.view(0, [2048], F32), W.view(8192, [2048], F32)]
        wv_bf = W.view(61440, [16, 512], BF16)
        vout = [W.view(77824, [512], BF16), W.view(78848, [512], BF16)]
        kv = d_kvst.ap()
        plan.wait("tensor", SC)
        for vq in range(4):
            for dc in range(NDC):
                par = dc % 2
                plan.wait("sync", VE)
                plan.dma("sync", lambda e, dc=dc, vq=vq, par=par:
                         e.dma_start(out=wsl[par][:, 0:512],
                                     in_=d_wv[dc][:, vq * 512:(vq + 1) * 512]),
                         "wsl%d" % par)
                plan.wait("vector", "wsl%d" % par)
                if vq > 0 and dc == 0:
                    plan.wait("vector", PE)
                plan.run("vector", lambda e, dc=dc, par=par:
                         e.tensor_copy(wv_bf[:, dc, :], wsl[par][:, 0:512]), inc=VE)
            plan.wait("tensor", VE)
            for tc in range(NDC):
                par = tc % 2
                ps_v = psum[:, par * 512:(par + 1) * 512]
                if vq * NDC + tc >= 2:
                    plan.wait("tensor", SC)
                for dc in range(NDC):
                    plan.run("tensor", lambda e, tc=tc, dc=dc, ps_v=ps_v:
                             e.matmul(ps_v, lnx[:, dc, tc * 128:(tc + 1) * 128],
                                      wv_bf[:, dc, :], start=dc == 0, stop=dc == NDC - 1),
                             inc=(PE if dc == NDC - 1 else None))
                plan.wait("scalar", PE)
                plan.wait("scalar", "vo%d" % par)
                plan.run("scalar", lambda e, par=par, ps_v=ps_v:
                         e.activation(vout[par], ps_v, AF.Copy), inc=SC)
                plan.wait("gpsimd", SC)
                kvv = kv[2048:4096, :].rearrange("(h p two) f -> h p two f",
                                                 h=8, p=P)
                for sub in range(2):
                    plan.dma("gpsimd", lambda e, tc=tc, vq=vq, par=par, sub=sub, kvv=kvv:
                             e.dma_start(
                                 out=kvv[2 * vq + sub][:, tc // 8,
                                                       (tc % 8) * 256:(tc % 8 + 1) * 256],
                                 in_=vout[par][:, sub * 256:(sub + 1) * 256]),
                             "vo%d" % par)

    def emit_qkv():
        kv = d_kvst.ap()
        emit_proj_qk(d_wk,
                     lambda of: kv[of * 128:(of + 1) * 128, :],
                     lambda of: kv[4096 + of:4097 + of, :])
        emit_vproj()
        for s in ("rrow", "ob0", "ob1", "vo0", "vo1"):
            plan.wait("gpsimd", s)
        plan.run("gpsimd", lambda e: e.collective_compute(
            "AllGather", OP.bypass,
            replica_groups=[[0, 1], [2, 3], [4, 5], [6, 7]],
            ins=[d_kvst.ap().opt()], outs=[d_kvg.ap().opt()]), inc=CC, dma=True)
        q = d_qst.ap()
        emit_proj_qk(d_wq,
                     lambda of: q[of * 128:(of + 1) * 128, :],
                     lambda of: d_rq[of:of + 1, :])
        barrier()

    def emit_attention():
        q1h = W.view(0, [2048], BF16)
        q2h = W.view(4096, [2048], BF16)
        k1h = W.view(8192, [2, 2048], BF16)
        k2h = W.view(16384, [2, 2048], BF16)
        vh = W.view(24576, [2, 16, 256], BF16)
        bc_q = W.view(40960, [2, 2048], BF16)
        bc_k1 = W.view(49152, [2, 2048], BF16)
        bc_k2 = W.view(57344, [2, 2048], BF16)
        maskslot = [W.view(65536, [4, 512], BF16), W.view(69632, [4, 512], BF16)]
        Et = [W.view(73728, [512], BF16), W.view(74752, [512], BF16)]
        bc_fin = W.view(75776, [512], BF16)
        rowb = W.view(76800, [512], BF16)
        rrow = W.view(77824, [512], F32)
        t_a = W.view(79872, [256], F32)
        t_b = W.view(80896, [256], F32)
        kvg = d_kvg.ap()
        kvg_r = kvg.rearrange("(r a) f -> r a f", r=2)
        qst = d_qst.ap()

        exp_marks = []
        pvm_marks = []
        cons_marks = []
        fin_marks = []
        step_idx = 0

        for h in range(NH):
            if h > 0:
                plan.wait("sync", VE, fin_marks[-1])
            plan.dma("sync", lambda e, h=h: e.dma_start(
                out=q1h, in_=qst[h * 256:h * 256 + 128, :]), "attld")
            plan.dma("sync", lambda e, h=h: e.dma_start(
                out=q2h, in_=qst[h * 256 + 128:h * 256 + 256, :]), "attld")
            for r in range(2):
                base = KVROWS * r
                plan.dma("sync", lambda e, h=h, base=base: e.dma_start(
                    out=k1h[:, (base // KVROWS), :],
                    in_=kvg[base + h * 256:base + h * 256 + 128, :]), "attld")
                plan.dma("sync", lambda e, h=h, base=base: e.dma_start(
                    out=k2h[:, (base // KVROWS), :],
                    in_=kvg[base + h * 256 + 128:base + h * 256 + 256, :]), "attld")
                plan.dma("sync", lambda e, h=h, r=r, base=base: e.dma_start(
                    out=vh[:, r, :, :].rearrange("p a f -> p (a f)"),
                    in_=kvg[base + 2048:base + 4096, :]
                        .rearrange("(hh p two) f -> hh p (two f)",
                                   hh=8, p=P)[h]), "attld")
            plan.wait("gpsimd", VE)
            plan.dma("gpsimd", lambda e, h=h: e.dma_start(
                out=bc_q, in_=d_rq[2 * h:2 * h + 2, :].unsqueeze(0)
                .broadcast_to([P, 2, 2048])), "bc")
            plan.dma("gpsimd", lambda e, h=h: e.dma_start(
                out=bc_k1, in_=kvg_r[:, 4096 + 2 * h, :].unsqueeze(0)
                .broadcast_to([P, 2, 2048])), "bc")
            plan.dma("gpsimd", lambda e, h=h: e.dma_start(
                out=bc_k2, in_=kvg_r[:, 4096 + 2 * h + 1, :].unsqueeze(0)
                .broadcast_to([P, 2, 2048])), "bc")
            plan.wait("vector", "attld")
            plan.wait("vector", "bc")
            if h > 0:
                plan.wait("vector", PE)
            plan.run("vector", lambda e: e.tensor_mul(q1h, q1h, bc_q[:, 0, :]))
            plan.run("vector", lambda e: e.tensor_mul(q2h, q2h, bc_q[:, 1, :]))
            plan.run("vector", lambda e: e.tensor_mul(
                k1h.rearrange("p a b -> p (a b)"), k1h.rearrange("p a b -> p (a b)"),
                bc_k1.rearrange("p a b -> p (a b)")))
            plan.run("vector", lambda e: e.tensor_mul(
                k2h.rearrange("p a b -> p (a b)"), k2h.rearrange("p a b -> p (a b)"),
                bc_k2.rearrange("p a b -> p (a b)")), inc=VE)
            norm_mark = plan.cnt[VE]

            for g in range(NG):
                gpar = g % 2
                gi = h * NG + g
                ps_r12 = psum[0:1, 1024 + gpar * 512:1024 + (gpar + 1) * 512]
                ps_y = psum[:, 2048 + gpar * 1024:2048 + (gpar + 1) * 1024]
                if gi >= 2:
                    plan.wait("sync", VE, fin_marks[gi - 2])
                plan.dma("sync", lambda e, g=g, gpar=gpar: e.dma_start(
                    out=maskslot[gpar], in_=d_mask[g]), "mk%d" % gpar)
                mask_ld_mark = plan.cnt["mk%d" % gpar]
                nsteps = 4 * g + 4
                qsl = slice(g * 256, (g + 1) * 256)
                first_group_wait = fin_marks[gi - 2] if gi >= 2 else 0
                for c in range(nsteps):
                    spar = step_idx % 2
                    ps_s = psum[:, spar * 512:(spar + 1) * 512]
                    rr, ll = owner_and_local(c)
                    if step_idx >= 2:
                        plan.wait("tensor", SC, exp_marks[step_idx - 2])
                    if c == 0:
                        plan.wait("tensor", VE, norm_mark)
                        if first_group_wait:
                            plan.wait("tensor", VE, first_group_wait)
                    plan.run("tensor", lambda e, ps_s=ps_s, rr=rr, ll=ll, qsl=qsl:
                             e.matmul(ps_s[:, 0:256], k1h[:, rr, ll * 128:(ll + 1) * 128],
                                      q1h[:, qsl], start=True, stop=True))
                    plan.run("tensor", lambda e, ps_s=ps_s, rr=rr, ll=ll, qsl=qsl:
                             e.matmul(ps_s[:, 256:512], k2h[:, rr, ll * 128:(ll + 1) * 128],
                                      q2h[:, qsl], start=True, stop=True), inc=PE)
                    s_mark = plan.cnt[PE]
                    plan.wait("scalar", PE, s_mark)
                    if step_idx >= 2:
                        plan.wait("scalar", PE, pvm_marks[step_idx - 2])
                        cm = cons_marks[step_idx - 2]
                        if cm[0] == VE:
                            pass
                    plan.run("scalar", lambda e, spar=spar, ps_s=ps_s:
                             e.activation(Et[spar], ps_s, AF.Exp,
                                          scale=sexp_col[:, :], bias=bexp_col[:, :]),
                             inc=SC)
                    exp_marks.append(plan.cnt[SC])
                    if c >= 4 * g:
                        plan.wait("vector", SC, exp_marks[-1])
                        plan.wait("vector", "mk%d" % gpar, mask_ld_mark)
                        plan.run("vector", lambda e, spar=spar, gpar=gpar, j=c - 4 * g:
                                 e.tensor_mul(Et[spar], Et[spar], maskslot[gpar][:, j, :]),
                                 inc=VE)
                        cons_marks.append((VE, plan.cnt[VE]))
                        plan.wait("tensor", VE, plan.cnt[VE])
                    else:
                        cons_marks.append((SC, exp_marks[-1]))
                        plan.wait("tensor", SC, exp_marks[-1])
                    first, last = c == 0, c == nsteps - 1
                    plan.run("tensor", lambda e, spar=spar, ps_r12=ps_r12, first=first,
                             last=last:
                             e.matmul(ps_r12, ones_bf[:, :], Et[spar],
                                      start=first, stop=last))
                    for half in range(2):
                        vsl = slice(half * 128, (half + 1) * 128)
                        plan.run("tensor", lambda e, rr=rr, ll=ll, vsl=vsl, half=half,
                                 spar=spar, ps_y=ps_y, first=first, last=last:
                                 e.matmul(ps_y[:, half * 512:(half + 1) * 512],
                                          vh[:, rr, ll, vsl], Et[spar][:, :],
                                          start=first, stop=last),
                                 inc=(PE if half == 1 else None))
                    pvm_marks.append(plan.cnt[PE])
                    step_idx += 1
                plan.wait("vector", PE, pvm_marks[-1])
                plan.run("vector", lambda e, ps_r12=ps_r12:
                         e.reciprocal(rrow[0:1, 0:256], ps_r12[:, 0:256]))
                plan.run("vector", lambda e, ps_r12=ps_r12:
                         e.reciprocal(rrow[0:1, 256:512], ps_r12[:, 256:512]))
                plan.run("vector", lambda e: e.tensor_scalar(
                    rrow[0:1, 256:512], rrow[0:1, 256:512], lamv[0:1, :], None, OP.mult))
                plan.run("vector", lambda e: e.tensor_copy(rowb[0:1, :], rrow[0:1, :]),
                         inc=VE)
                plan.wait("gpsimd", VE)
                fpar = gi % 2
                plan.dma("gpsimd", lambda e, fpar=fpar: e.dma_start(
                    out=d_finrows[fpar:fpar + 1, :], in_=rowb[0:1, :]), "fr")
                plan.wait("gpsimd", "fr")
                plan.dma("gpsimd", lambda e, fpar=fpar: e.dma_start(
                    out=bc_fin, in_=d_finrows[fpar:fpar + 1, :].broadcast_to([P, 512])),
                    "fr")
                plan.wait("vector", "fr")
                for half in range(2):
                    y1 = 2 * half
                    y2 = 2 * half + 1
                    fc = 2 * h + half
                    plan.run("vector", lambda e, y1=y1, ps_y=ps_y:
                             e.tensor_mul(t_a, ps_y[:, y1 * 256:(y1 + 1) * 256],
                                          bc_fin[:, 0:256]))
                    plan.run("vector", lambda e, y2=y2, ps_y=ps_y:
                             e.tensor_mul(t_b, ps_y[:, y2 * 256:(y2 + 1) * 256],
                                          bc_fin[:, 256:512]))
                    plan.run("vector", lambda e, fc=fc, qsl=qsl:
                             e.tensor_sub(yf[:, fc, qsl], t_a, t_b),
                             inc=(VE if half == 1 else None))
                fin_marks.append(plan.cnt[VE])
        barrier()

    def emit_oproj():
        wsl = [W.view(0, [2048], F32), W.view(8192, [2048], F32)]
        wslb = [W.view(16384, [2048], BF16), W.view(20480, [2048], BF16)]
        st_res = [W.view(24576, [2048], F32), W.view(32768, [2048], F32)]
        st_out = [W.view(40960, [2048], F32), W.view(49152, [2048], F32)]
        evac_marks = []
        for do in range(NDC):
            par = do % 2
            if do >= 2:
                plan.wait("sync", VE)
            plan.dma("sync", lambda e, do=do, par=par:
                     e.dma_start(out=wsl[par], in_=d_wo[do]), "wsl%d" % par)
            plan.dma("sync", lambda e, do=do, par=par:
                     e.dma_start(out=st_res[par], in_=d_stream[do]), "sr%d" % par)
            plan.wait("vector", "wsl%d" % par)
            if do >= 2:
                plan.wait("vector", PE)
            plan.run("vector", lambda e, par=par: e.tensor_copy(wslb[par], wsl[par]),
                     inc=VE)
            ps_o = psum[:, par * 2048:(par + 1) * 2048]
            plan.wait("tensor", VE)
            if do >= 2:
                plan.wait("tensor", VE, evac_marks[do - 2])
            for fc in range(NDC):
                for ts in range(4):
                    sl = slice(ts * 512, (ts + 1) * 512)
                    plan.run("tensor", lambda e, par=par, fc=fc, sl=sl, ps_o=ps_o:
                             e.matmul(ps_o[:, sl], wslb[par][:, fc * 128:(fc + 1) * 128],
                                      yf[:, fc, sl], start=fc == 0, stop=fc == NDC - 1),
                             inc=(PE if (fc == NDC - 1 and ts == 3) else None))
            plan.wait("vector", PE)
            plan.wait("vector", "sr%d" % par)
            if do >= 2:
                plan.wait("vector", "so%d" % par)
            plan.run("vector", lambda e, par=par, ps_o=ps_o:
                     e.scalar_tensor_tensor(st_out[par], ps_o, 0.2, st_res[par],
                                            OP.mult, OP.add), inc=VE)
            evac_marks.append(plan.cnt[VE])
            plan.wait("gpsimd", VE)
            plan.dma("gpsimd", lambda e, do=do, par=par:
                     e.dma_start(out=d_stream[do], in_=st_out[par]), "so%d" % par)
        barrier()

    if stop_after == "prologue":
        blk = nc.Block("main")
        with blk:
            for deco, eng_name in ((blk.sync, "sync"), (blk.gpsimd, "gpsimd"),
                                   (blk.tensor, "tensor"), (blk.scalar, "scalar"),
                                   (blk.vector, "vector")):
                def mk(ops):
                    def body(e):
                        for f in ops:
                            f(e)
                    return body
                deco(mk(plan.ops[eng_name]))
        return nc
    emit_ln(d_x)
    if stop_after == "dumpln":
        dtile = W.view(0, [2048], F32)
        for dc in range(16):
            plan.run("vector", lambda e, dc=dc: e.tensor_copy(dtile, lnx[:, dc, :]),
                     inc=VE)
            plan.wait("gpsimd", VE)
            plan.dma("gpsimd", lambda e, dc=dc: e.dma_start(out=d_out[dc], in_=dtile),
                     "dmps")
            barrier()
        blk = nc.Block("main")
        with blk:
            for deco, eng_name in ((blk.sync, "sync"), (blk.gpsimd, "gpsimd"),
                                   (blk.tensor, "tensor"), (blk.scalar, "scalar"),
                                   (blk.vector, "vector")):
                def mk(ops):
                    def body(e):
                        for f in ops:
                            f(e)
                    return body
                deco(mk(plan.ops[eng_name]))
        return nc
    if stop_after == "ln":
        barrier()
        blk = nc.Block("main")
        with blk:
            for deco, eng_name in ((blk.sync, "sync"), (blk.gpsimd, "gpsimd"),
                                   (blk.tensor, "tensor"), (blk.scalar, "scalar"),
                                   (blk.vector, "vector")):
                def mk(ops):
                    def body(e):
                        for f in ops:
                            f(e)
                    return body
                deco(mk(plan.ops[eng_name]))
        return nc
    emit_ffn_a()
    if stop_after == "dumpln2":
        dtile2 = W.view(0, [2048], F32)
        for dc in range(16):
            plan.run("vector", lambda e, dc=dc: e.tensor_copy(dtile2, lnx[:, dc, :]),
                     inc=VE)
            plan.wait("gpsimd", VE)
            plan.dma("gpsimd", lambda e, dc=dc: e.dma_start(out=d_out[dc], in_=dtile2),
                     "dmps")
            barrier()
        blk = nc.Block("main")
        with blk:
            for deco, eng_name in ((blk.sync, "sync"), (blk.gpsimd, "gpsimd"),
                                   (blk.tensor, "tensor"), (blk.scalar, "scalar"),
                                   (blk.vector, "vector")):
                def mk(ops):
                    def body(e):
                        for f in ops:
                            f(e)
                    return body
                deco(mk(plan.ops[eng_name]))
        return nc
    if stop_after == "dumpfuse":
        btile = W.view(0, [4, 512], BF16)
        dtile = W.view(8192, [2048], F32)
        for i in range(16):
            plan.dma("sync", lambda e, i=i: e.dma_start(
                out=btile, in_=d_fuse[0][:, 4 * i:4 * i + 4, :]), "dmp")
            plan.wait("vector", "dmp")
            plan.run("vector", lambda e: e.tensor_copy(
                dtile, btile.rearrange("p a b -> p (a b)")), inc=VE)
            plan.wait("gpsimd", VE)
            plan.dma("gpsimd", lambda e, i=i: e.dma_start(out=d_out[i], in_=dtile),
                     "dmps")
            barrier()
        blk = nc.Block("main")
        with blk:
            for deco, eng_name in ((blk.sync, "sync"), (blk.gpsimd, "gpsimd"),
                                   (blk.tensor, "tensor"), (blk.scalar, "scalar"),
                                   (blk.vector, "vector")):
                def mk(ops):
                    def body(e):
                        for f in ops:
                            f(e)
                    return body
                deco(mk(plan.ops[eng_name]))
        return nc
    if stop_after == "ffna2":
        emit_ffn_a()
        barrier()
        blk = nc.Block("main")
        with blk:
            for deco, eng_name in ((blk.sync, "sync"), (blk.gpsimd, "gpsimd"),
                                   (blk.tensor, "tensor"), (blk.scalar, "scalar"),
                                   (blk.vector, "vector")):
                def mk(ops):
                    def body(e):
                        for f in ops:
                            f(e)
                    return body
                deco(mk(plan.ops[eng_name]))
        return nc
    if stop_after == "ffna":
        barrier()
        blk = nc.Block("main")
        with blk:
            for deco, eng_name in ((blk.sync, "sync"), (blk.gpsimd, "gpsimd"),
                                   (blk.tensor, "tensor"), (blk.scalar, "scalar"),
                                   (blk.vector, "vector")):
                def mk(ops):
                    def body(e):
                        for f in ops:
                            f(e)
                    return body
                deco(mk(plan.ops[eng_name]))
        return nc
    if stop_after.startswith("ffnb:"):
        _, a, b = stop_after.split(":")
        emit_ffn_b(d_x, d_out, int(a), int(b))
    elif stop_after == "ffn1":
        emit_ffn_b(d_x, d_out)
    else:
        emit_ffn_b(d_x, d_stream)
        emit_ln(d_stream)
        emit_qkv()
        emit_attention()
        emit_oproj()
        if stop_after == "attn":
            tmp = W.view(0, [2048], F32)
            for i in range(16):
                plan.dma("sync", lambda e, i=i: e.dma_start(out=tmp, in_=d_stream[i]),
                         "dmp")
                plan.wait("gpsimd", "dmp")
                plan.dma("gpsimd", lambda e, i=i: e.dma_start(out=d_out[i], in_=tmp),
                         "dmps")
                barrier()
        else:
            emit_ln(d_stream)
            emit_ffn_a()
            emit_ffn_b(d_stream, d_out)

    barrier()

    blk = nc.Block("main")
    with blk:
        for deco, eng_name in ((blk.sync, "sync"), (blk.gpsimd, "gpsimd"),
                               (blk.tensor, "tensor"), (blk.scalar, "scalar"),
                               (blk.vector, "vector")):
            def make_body(ops):
                def body(e):
                    for f in ops:
                        f(e)
                return body
            deco(make_body(plan.ops[eng_name]))
    return nc


_NC_CACHE = {}


def get_nc(stop_after="full"):
    if stop_after not in _NC_CACHE:
        _NC_CACHE[stop_after] = build_program(stop_after)
    return _NC_CACHE[stop_after]


def kernel(**inputs):
    in_maps = stage_inputs(inputs)
    nc = get_nc("full")
    res = run_bass_kernel_spmd(nc, in_maps, core_ids=list(range(8)))
    return assemble_output(res.results)
